# version 15
# baseline (speedup 1.0000x reference)
import contextlib
import numpy as np
import ml_dtypes
import concourse.bass as bass
import concourse.mybir as mybir
from concourse.bass_utils import run_bass_kernel_spmd

F32 = mybir.dt.float32
BF16 = mybir.dt.bfloat16
AF = mybir.ActivationFunctionType
ALU = mybir.AluOpType
AX = mybir.AxisListType
BF = ml_dtypes.bfloat16

D, DC, HH, HC, NH, E = 1024, 8, 4096, 32, 16, 64
NCORES = 8
ENGINES = ("tensor", "vector", "scalar", "sync", "gpsimd")


class Plan:

    def __init__(self, nc, es):
        self.nc, self.es = nc, es
        self.ops = {k: [] for k in ENGINES}
        self.sems, self.cnt, self.memo = {}, {}, {}

    def sem(self, *names):
        for n in names:
            self.sems[n] = self.es.enter_context(self.nc.semaphore(n))
            self.cnt[n] = 0

    def emit(self, eng, fn, waits=(), inc=None):
        wl = []
        for s, v in waits:
            if v and v > 0 and self.memo.get((eng, s), -1) < v:
                wl.append((self.sems[s], v))
                self.memo[(eng, s)] = v
        rec_inc = None
        if inc is not None:
            s, k = inc
            self.cnt[s] += k
            rec_inc = (self.sems[s], k)
        self.ops[eng].append((fn, wl, rec_inc))
        return self.cnt[inc[0]] if inc else None

    def dma(self, eng, out, in_, sem, waits=()):
        return self.emit(eng, lambda e, o=out, i=in_: e.dma_start(out=o, in_=i),
                         waits, (sem, 16))

    def barrier(self):
        return {s: self.cnt[s] for s in ("sT", "sV", "sS")}

    def replay(self, block):
        for eng in ENGINES:
            recs = self.ops[eng]
            if not recs:
                continue
            def run(e, recs=recs):
                for fn, wl, inc in recs:
                    for h, v in wl:
                        e.wait_ge(h, v)
                    inst = fn(e)
                    if inc is not None:
                        inst.then_inc(inc[0], inc[1])
            getattr(block, eng)(run)


def build_program(cfg):
    S = cfg['S']
    NSEQ = cfg['NSEQ']
    SW = min(1024, S)
    NSTRIP = S // SW
    IT = S // 128
    NQ = S // 512
    assert NQ in (1, 4), 'S must be 512 or 2048'
    NHW = SW // 512
    dbg_after = cfg.get('dbg_after')

    nc = bass.Bass("TRN2", target_bir_lowering=False,
                   detect_race_conditions=False)

    d_x = nc.dram_tensor("x", [NSEQ, D, S], F32, kind="ExternalInput").ap()
    d_wfcT = nc.dram_tensor("wfcT", [D, HH], F32, kind="ExternalInput").ap()
    d_wprT = nc.dram_tensor("wprT", [HH, D], F32, kind="ExternalInput").ap()
    d_wq = nc.dram_tensor("wq", [D, NH * E], F32, kind="ExternalInput").ap()
    d_wk = nc.dram_tensor("wk", [D, NH * E], F32, kind="ExternalInput").ap()
    d_wv = nc.dram_tensor("wv", [D, NH * E], F32, kind="ExternalInput").ap()
    d_woT = nc.dram_tensor("woT", [D, D], F32, kind="ExternalInput").ap()
    d_bq = nc.dram_tensor("bq", [128, NH // 2], F32, kind="ExternalInput").ap()
    d_bk = nc.dram_tensor("bk", [128, NH // 2], F32, kind="ExternalInput").ap()
    d_bv = nc.dram_tensor("bv", [128, NH // 2], F32, kind="ExternalInput").ap()
    d_bo = nc.dram_tensor("bo", [128, DC], F32, kind="ExternalInput").ap()
    d_lnw = nc.dram_tensor("lnw", [128, HC], F32, kind="ExternalInput").ap()
    d_lnb = nc.dram_tensor("lnb", [128, HC], F32, kind="ExternalInput").ap()
    d_idf = nc.dram_tensor("idf", [128, 128], F32, kind="ExternalInput").ap()
    d_idb = nc.dram_tensor("idb", [128, 128], BF16, kind="ExternalInput").ap()
    d_onesb = nc.dram_tensor("onesb", [128, 128], BF16, kind="ExternalInput").ap()
    d_eps = nc.dram_tensor("eps", [128, 1], F32, kind="ExternalInput").ap()
    d_lnhalf = nc.dram_tensor("lnhalf", [128, 1], F32, kind="ExternalInput").ap()
    d_y = nc.dram_tensor("y", [NSEQ, D, S], F32, kind="ExternalOutput").ap()
    d_dbg = (nc.dram_tensor("dbg", [NSEQ, D, S], F32, kind="ExternalOutput").ap()
             if dbg_after else None)
    d_qfc = nc.dram_tensor("qfc", [HC, 128, D], BF16, kind="Internal").ap()
    if cfg.get('dbg_qfc'):
        d_dbgq = nc.dram_tensor("dbgq", [HC, 128, D], BF16, kind="ExternalOutput").ap()
        d_dbgp = nc.dram_tensor("dbgp", [DC, 128, HH], BF16, kind="ExternalOutput").ap()
        d_dbgc = nc.dram_tensor("dbgc", [128, 2 * DC + 4], F32, kind="ExternalOutput").ap()
    d_qpr = nc.dram_tensor("qpr", [DC, 128, HH], BF16, kind="Internal").ap()
    d_row = nc.dram_tensor("rowd", [1, S], BF16, kind="Internal").ap()
    d_g1 = nc.dram_tensor("gam", [1, 2], F32, kind="Internal").ap()
    d_arow = nc.dram_tensor("arow", [IT, 128], F32, kind="Internal").ap()

    es = contextlib.ExitStack()
    sbf = lambda n, sh, dt: es.enter_context(nc.sbuf_tensor(n, sh, dt))

    x_t = [sbf(f"x{c}", [128, S], F32) for c in range(DC)]
    xn_t = [sbf(f"xn{c}", [128, S], BF16) for c in range(DC)]
    c_idf = sbf("c_idf", [128, 128], F32)
    c_idb = sbf("c_idb", [128, 128], BF16)
    c_onesb = sbf("c_onesb", [128, 128], BF16)
    c_eps = sbf("c_eps", [128, 1], F32)
    c_lnhalf = sbf("c_lnhalf", [128, 1], F32)
    c_onesf = sbf("c_onesf", [128, 2], F32)
    p_bq = sbf("p_bq", [128, NH // 2], F32)
    p_bk = sbf("p_bk", [128, NH // 2], F32)
    p_bv = sbf("p_bv", [128, NH // 2], F32)
    p_bo = sbf("p_bo", [128, DC], F32)
    p_lnw = sbf("p_lnw", [128, HC], F32)
    p_lnb = sbf("p_lnb", [128, HC], BF16)
    c1_t = sbf("c1", [128, DC], F32)
    c2sb_t = sbf("c2sb", [128, DC], F32)
    tau_t = sbf("tau", [128, 2], F32)
    ntau_t = sbf("ntau", [128, 2], F32)
    sc1_t = sbf("sc1", [1, 2], F32)
    LNW = SW

    pA = es.enter_context(nc.psum_tensor("pA", [128, 1024], F32))
    pB = es.enter_context(nc.psum_tensor("pB", [128, 1024], F32))

    P = Plan(nc, es)
    P.sem("sT", "sV", "sS",
          "dX", "dXO", "dWA", "dWB", "dWC", "dWD", "dC", "dPA", "dPB", "dSCR",
          "gA", "gB", "gS", "gW")

    for dst, src in [(c_idf[:], d_idf), (c_idb[:], d_idb), (c_onesb[:], d_onesb),
                     (c_eps[:], d_eps), (c_lnhalf[:], d_lnhalf),
                     (p_bq[:], d_bq), (p_bk[:], d_bk), (p_bv[:], d_bv),
                     (p_bo[:], d_bo), (p_lnw[:], d_lnw)]:
        cD = P.dma("sync", dst, src, "dC")
    for c in range(DC):
        xload0 = P.dma("sync", x_t[c][:], d_x[0, 128 * c:128 * (c + 1), :], "dX")
    cG = P.dma("gpsimd", p_lnb[:], d_lnb, "gA")
    P.emit("vector", lambda e: e.tensor_scalar_mul(p_bq[:], p_bq[:], 0.125),
           [("dC", cD)], ("sV", 1))
    consts_v = P.emit("vector", lambda e: e.memset(c_onesf[:], 1.0), (), ("sV", 1))

    NCH = DC + HC
    wraw_es = contextlib.ExitStack()
    wraw_b = [wraw_es.enter_context(nc.sbuf_tensor(f"wraw{i}", [128, HH], F32))
              for i in range(2)]
    wqc_b = [wraw_es.enter_context(nc.sbuf_tensor(f"wqc{i}", [128, HH], BF16))
             for i in range(2)]
    negq_t = wraw_es.enter_context(nc.sbuf_tensor("negq", [128, HH], F32))
    gacc_t = wraw_es.enter_context(nc.sbuf_tensor("gacc", [128, 512], F32))
    gred_t = wraw_es.enter_context(nc.sbuf_tensor("gred", [128, 2], F32))
    gred2_t = wraw_es.enter_context(nc.sbuf_tensor("gred2", [2, 128], F32))

    def chunk_src(i):
        if i < DC:
            return d_wfcT[128 * i:128 * (i + 1), :], HH
        return d_wprT[128 * (i - DC):128 * (i - DC + 1), :], D

    p1v = {}
    seg_base = {0: 0, 1: 64}
    for i in range(NCH):
        semn = "dPA" if i % 2 == 0 else "dPB"
        src, w = chunk_src(i)
        dv = P.dma("sync", wraw_b[i % 2][:, 0:w], src, semn,
                   [("sV", p1v.get(i - 2, 0))])
        nseg = w // 512
        base = (i * 8) if i < DC else (64 + (i - DC) * 2)
        for s_ in range(nseg):
            p1v[i] = P.emit("vector",
                            lambda e, i=i, s_=s_, base=base: e.reduce_sum(
                                gacc_t[:, base + s_:base + s_ + 1],
                                wraw_b[i % 2][:, 512 * s_:512 * (s_ + 1)],
                                axis=AX.X, apply_absolute_value=True),
                            [(semn, dv)] if s_ == 0 else (),
                            ("sV", 1) if s_ == nseg - 1 else None)

    def vnop():
        P.emit("vector", lambda e: e.engine_nop())

    def tree_cols(dst_off, src_off, n):
        cur_off, cur_n = src_off, n
        while cur_n > 1:
            half = cur_n // 2
            vnop(); vnop()
            P.emit("vector", lambda e, a=cur_off, h=half: e.tensor_add(
                gacc_t[:, a:a + h], gacc_t[:, a:a + h], gacc_t[:, a + h:a + 2 * h]))
            if cur_n % 2:
                vnop(); vnop()
                P.emit("vector", lambda e, a=cur_off, h=half, cn=cur_n: e.tensor_add(
                    gacc_t[:, a:a + 1], gacc_t[:, a:a + 1],
                    gacc_t[:, a + cn - 1:a + cn]))
            cur_n = half
        if cur_off != dst_off:
            vnop(); vnop()
            P.emit("vector", lambda e, a=cur_off, d=dst_off: e.tensor_copy(
                gacc_t[:, d:d + 1], gacc_t[:, a:a + 1]))

    P.emit("vector", lambda e: e.engine_nop(), [("sV", p1v[NCH - 1])])
    tree_cols(400, 0, 64)
    tree_cols(401, 64, 64)
    p1red = P.emit("vector", lambda e: e.engine_nop(), (), ("sV", 1))
    p1t = P.emit("tensor", lambda e: e.transpose(
        pA[0:2, 0:128], gacc_t[:, 400:402], c_idf[:]),
        [("sV", p1red), ("sV", consts_v), ("dC", cD)], ("sT", 1))
    p1c = P.emit("vector", lambda e: e.tensor_copy(gred2_t[:], pA[0:2, 0:128]),
                 [("sT", p1t)], ("sV", 1))
    cur = 128
    while cur > 1:
        half = cur // 2
        vnop(); vnop()
        P.emit("vector", lambda e, h=half: e.tensor_add(
            gred2_t[:, 0:h], gred2_t[:, 0:h], gred2_t[:, h:2 * h]))
        cur = half
    vnop(); vnop()
    p1s = P.emit("vector", lambda e: e.engine_nop(), (), ("sV", 1))
    p1t2 = P.emit("tensor", lambda e: e.transpose(
        pA[0:1, 0:2], gred2_t[:, 0:1], c_idf[0:2, 0:2]), [("sV", p1s)], ("sT", 1))
    P.emit("vector", lambda e: e.tensor_scalar_mul(sc1_t[0:1, 0:2], pA[0:1, 0:2],
                                                   1.0 / (D * HH)), [("sT", p1t2)])
    vnop(); vnop()
    P.emit("vector", lambda e: e.tensor_scalar(sc1_t[0:1, 0:2], sc1_t[0:1, 0:2],
                                               1e-6, None, ALU.add))
    vnop(); vnop()
    ptau = P.emit("vector", lambda e: e.tensor_scalar_mul(sc1_t[0:1, 0:2],
                                                          sc1_t[0:1, 0:2], 0.5),
                  (), ("sV", 1))
    g1 = P.dma("gpsimd", d_g1[0:1, 0:2], sc1_t[0:1, 0:2], "gS", [("sV", ptau)])
    g2 = P.dma("gpsimd", tau_t[:, 0:2], d_g1[0:1, 0:2].partition_broadcast(128),
               "gS", [("gS", g1)])
    pntau = P.emit("vector", lambda e: e.tensor_scalar_mul(ntau_t[:, 0:2],
                                                           tau_t[:, 0:2], -1.0),
                   [("gS", g2)], ("sV", 1))

    prep_scr_fc_holder = [0]
    p2v, p2q, p2c2t, p2c1t, p2s = {}, {}, {}, {}, {}
    first_c2 = [True]
    for i in range(NCH):
        fc = i < DC
        semn = "dPA" if i % 2 == 0 else "dPB"
        src, w = chunk_src(i)
        dv = P.dma("sync", wraw_b[i % 2][:, 0:w], src, semn,
                   [("sV", p2v.get(i - 2, p1red))])
        col = 0 if fc else 1
        P.emit("vector",
               lambda e, i=i, w=w, col=col: e.tensor_single_scalar(
                   negq_t[:, 0:w], wraw_b[i % 2][:, 0:w],
                   ntau_t[:, col:col + 1], ALU.is_lt),
               [(semn, dv), ("sV", pntau)])
        p2q[i] = P.emit("vector",
                        lambda e, i=i, w=w, col=col: e.scalar_tensor_tensor(
                            wqc_b[i % 2][:, 0:w], wraw_b[i % 2][:, 0:w],
                            tau_t[:, col:col + 1], negq_t[:, 0:w],
                            ALU.is_gt, ALU.subtract),
                        [("dSCR", p2s.get(i - 2, 0)),
                         ("sT", p2c1t.get(i - 2, 0))], ("sV", 1))
        if fc:
            p2v[i] = p2q[i]
        else:
            hc = i - DC
            for dM in range(DC):
                p2c2t[i] = P.emit(
                    "tensor",
                    lambda e, i=i, dM=dM, hc=hc, f=first_c2[0]: e.matmul(
                        pA[:, dM:dM + 1], wqc_b[i % 2][:, 128 * dM:128 * (dM + 1)],
                        p_lnb[:, hc:hc + 1], start=f, stop=False),
                    [("sV", p2q[i]), ("gA", cG)], ("sT", 1))
                first_c2[0] = False
            p2v[i] = P.emit("vector",
                            lambda e, i=i, hc=hc: e.tensor_single_scalar(
                                wqc_b[i % 2][:, 0:D], wqc_b[i % 2][:, 0:D],
                                p_lnw[:, hc:hc + 1], ALU.mult),
                            [("sT", p2c2t[i])], ("sV", 1))
            for dM in range(DC):
                last = (hc == HC - 1 and dM == DC - 1)
                p2c1 = P.emit(
                    "tensor",
                    lambda e, i=i, dM=dM, last=last: e.matmul(
                        pA[:, DC + dM:DC + dM + 1],
                        wqc_b[i % 2][:, 128 * dM:128 * (dM + 1)],
                        c_onesb[:, 0:1], start=False, stop=last),
                    [("sV", p2v[i])], ("sT", 1))
            p2c1t[i] = p2c1
        if fc:
            for M in range(HC):
                scr = P.dma("sync", d_qfc[M, :, 128 * i:128 * (i + 1)],
                            wqc_b[i % 2][:, 128 * M:128 * (M + 1)], "dSCR",
                            [("sV", p2v[i])])
        else:
            hc = i - DC
            for M in range(DC):
                scr = P.dma("sync", d_qpr[M, :, 128 * hc:128 * (hc + 1)],
                            wqc_b[i % 2][:, 128 * M:128 * (M + 1)], "dSCR",
                            [("sV", p2v[i])])
        p2s[i] = P.cnt["dSCR"]
        if i == DC - 1:
            prep_scr_fc_holder[0] = P.cnt["dSCR"]
    prep_scr = P.cnt["dSCR"]
    P.emit("vector", lambda e: e.tensor_scalar_mul(c2sb_t[:], pA[:, 0:DC], 0.5),
           [("sT", p2c1)])
    pc12 = P.emit("vector", lambda e: e.tensor_copy(c1_t[:], pA[:, DC:2 * DC]),
                  (), ("sV", 1))
    prep_done_V = pc12
    prep_scr_fc = prep_scr_fc_holder[0]
    wraw_es.close()

    wstate = {"dWA": 0, "dWB": 0, "dWC": 0, "dWD": 0}

    def outer_ln(first_waits):
        uid[0] += 1
        esL = contextlib.ExitStack()
        lt = f"L{uid[0]}"
        lsb = lambda n, sh, dt: esL.enter_context(nc.sbuf_tensor(n + lt, sh, dt))
        mu_t = lsb("mu", [128, LNW], F32)
        rstd_t = lsb("rstd", [128, LNW], F32)
        nmr_t = lsb("nmr", [128, LNW], F32)
        tmp1_t = lsb("tmp1", [128, LNW], F32)
        xc_t = [lsb(f"xc{i}", [128, LNW], BF16) for i in range(2)]
        xq_t = [lsb(f"xq{i}", [128, LNW], BF16) for i in range(2)]
        xn_done = 0
        prev_pvar = 0
        for half in range(S // LNW):
            cs = slice(LNW * half, LNW * (half + 1))
            vmk = {}
            tmk = {}
            for c in range(DC):
                w = list(first_waits) if (half == 0 and c == 0) else []
                w.append(("sT", tmk.get(c - 2, 0)))
                P.emit("vector", lambda e, c=c, cs=cs: e.tensor_copy(
                    xc_t[c % 2][:], x_t[c][:, cs]), w)
                vmk[c] = P.emit("vector", lambda e, c=c: e.tensor_mul(
                    xq_t[c % 2][:], xc_t[c % 2][:], xc_t[c % 2][:]), (), ("sV", 1))
                for w_ in range(NHW):
                    ws = slice(512 * w_, 512 * (w_ + 1))
                    P.emit("tensor", lambda e, c=c, ws=ws: e.matmul(
                        pA[:, ws], c_onesb[:], xc_t[c % 2][:, ws],
                        start=(c == 0), stop=(c == DC - 1)),
                        [("sV", vmk[c]), ("sV", prev_pvar)])
                    tmk[c] = P.emit("tensor", lambda e, c=c, ws=ws: e.matmul(
                        pB[:, ws], c_onesb[:], xq_t[c % 2][:, ws],
                        start=(c == 0), stop=(c == DC - 1)), (), ("sT", 1))
            L = slice(0, LNW)
            P.emit("vector", lambda e, L=L: e.tensor_scalar_mul(
                mu_t[:, L], pA[:, L], 1.0 / D), [("sT", tmk[DC - 1])])
            P.emit("vector", lambda e, L=L: e.tensor_mul(
                tmp1_t[:, L], mu_t[:, L], mu_t[:, L]))
            pvar = P.emit("vector", lambda e, L=L: e.scalar_tensor_tensor(
                rstd_t[:, L], pB[:, L], 1.0 / D, tmp1_t[:, L],
                ALU.mult, ALU.subtract), (), ("sV", 1))
            prev_pvar = pvar
            pstd = P.emit("scalar", lambda e, L=L: e.activation(
                tmp1_t[:, L], rstd_t[:, L], AF.Sqrt, bias=c_eps[:], scale=1.0),
                [("sV", pvar)], ("sS", 1))
            P.emit("vector", lambda e, L=L: e.reciprocal(rstd_t[:, L], tmp1_t[:, L]),
                   [("sS", pstd)])
            P.emit("vector", lambda e, L=L: e.scalar_tensor_tensor(
                nmr_t[:, L], mu_t[:, L], -1.0, rstd_t[:, L], ALU.mult, ALU.mult))
            for c in range(DC):
                P.emit("vector", lambda e, c=c, cs=cs, L=L: e.tensor_mul(
                    tmp1_t[:, L], x_t[c][:, cs], rstd_t[:, L]))
                xn_done = P.emit("vector", lambda e, c=c, cs=cs, L=L: e.tensor_add(
                    xn_t[c][:, cs], tmp1_t[:, L], nmr_t[:, L]), (), ("sV", 1))
        esL.close()
        return xn_done

    uid = [0]

    def ffn(seq, xwait, EARLY=2):
        uid[0] += 1
        tag = f"f{uid[0]}"
        bar = P.barrier()
        xn_done = outer_ln([("sT", bar["sT"]), ("dX", xwait)])
        esF = contextlib.ExitStack()
        pC = esF.enter_context(nc.psum_tensor(f"pC{tag}", [128, 1024], F32))
        pD = esF.enter_context(nc.psum_tensor(f"pD{tag}", [128, 1024], F32))
        g_t = [esF.enter_context(nc.sbuf_tensor(f"g{m}{tag}", [128, SW], BF16))
               for m in range(HC + EARLY)]
        tmp1_t = esF.enter_context(nc.sbuf_tensor(f"tmp1{tag}", [128, SW], F32))
        rstd_t = esF.enter_context(nc.sbuf_tensor(f"rstd{tag}", [128, SW], F32))
        nmr_t = esF.enter_context(nc.sbuf_tensor(f"nmr{tag}", [128, SW], F32))
        XQN = max(EARLY, 2) + 1
        xq_t = [esF.enter_context(nc.sbuf_tensor(f"xq{i}{tag}", [128, SW], BF16))
                for i in range(XQN)]
        wfc_b = [esF.enter_context(nc.sbuf_tensor(f"wfc{i}{tag}", [128, D], BF16))
                 for i in range(4)]
        wpr_b = [esF.enter_context(nc.sbuf_tensor(f"wpr{i}{tag}", [128, HH], BF16))
                 for i in range(2)]
        bufT, bufP = {}, {}
        gidx = lambda sp, M: M + (HC if (M < EARLY and sp % 2 == 1) else 0)
        w0 = dict(wstate)
        fcm, gel, gsq, stm = {}, {}, {}, {}
        prev_pje = {}
        chain_v = {}
        gel_last = {}
        evac_last = {}

        def emit_fc(sp, M):
            b = M % 4
            semn = "dWA" if M % 2 == 0 else "dWB"
            dv = P.dma("sync", wfc_b[b][:], d_qfc[M], semn,
                       [("sT", max(bufT.get(b, 0), w0[semn],
                                   bar["sT"])), ("dSCR", prep_scr_fc)])
            wts = [(semn, dv), ("sV", xn_done), ("sS", gel_last.get(M % 2, 0)),
                   ("sV", evac_last.get(M % 2, 0))]
            c0 = SW * sp
            for k in range(DC):
                for w_ in range(NHW):
                    ws = slice(512 * w_, 512 * (w_ + 1))
                    xw = slice(c0 + 512 * w_, c0 + 512 * (w_ + 1))
                    fcm[(sp, M)] = P.emit(
                        "tensor", lambda e, M=M, k=k, ws=ws, xw=xw: e.matmul(
                            (pA if M % 2 == 0 else pB)[:, ws],
                            wfc_b[M % 4][:, 128 * k:128 * (k + 1)],
                            xn_t[k][:, xw], start=(k == 0), stop=(k == DC - 1)),
                        wts if k == 0 and w_ == 0 else (),
                        ("sT", 1) if (k == DC - 1 and w_ == NHW - 1) else None)
            wstate[semn] = fcm[(sp, M)]
            bufT[b] = fcm[(sp, M)]
            gi = gidx(sp, M)
            gel[(sp, M)] = P.emit("scalar", lambda e, M=M, gi=gi: e.activation(
                g_t[gi][:], (pA if M % 2 == 0 else pB)[:, 0:SW], AF.Gelu),
                [("sT", fcm[(sp, M)])], ("sS", 1))
            gel_last[M % 2] = gel[(sp, M)]
            gsq[(sp, M)] = P.emit("vector", lambda e, M=M, gi=gi: e.tensor_mul(
                xq_t[M % XQN][:], g_t[gi][:], g_t[gi][:]),
                [("sS", gel[(sp, M)]), ("sT", stm.get((sp, M - XQN), 0))], ("sV", 1))

        def emit_stats(sp, M):
            gi = gidx(sp, M)
            w = [("sV", gsq[(sp, M)]),
                 ("sV", chain_v.get(sp - 1, 0) if M == 0 else 0)]
            for w_ in range(NHW):
                ws = slice(512 * w_, 512 * (w_ + 1))
                P.emit("tensor", lambda e, gi=gi, ws=ws: e.matmul(
                    pC[:, ws], c_onesb[:], g_t[gi][:, ws],
                    start=(M == 0), stop=(M == HC - 1)), w if w_ == 0 else ())
                stm[(sp, M)] = P.emit("tensor", lambda e, M=M, ws=ws: e.matmul(
                    pD[:, ws], c_onesb[:], xq_t[M % XQN][:, ws],
                    start=(M == 0), stop=(M == HC - 1)), (), ("sT", 1))

        ev_done = 0
        for sp in range(NSTRIP):
            c0 = SW * sp
            for M in range(HC):
                if not (sp > 0 and M < EARLY):
                    emit_fc(sp, M)
                emit_stats(sp, M)
            L = slice(0, SW)
            P.emit("vector", lambda e, L=L: e.tensor_scalar_mul(
                nmr_t[:, L], pC[:, L], 1.0 / HH), [("sT", stm[(sp, HC - 1)])])
            P.emit("vector", lambda e, L=L: e.tensor_mul(
                tmp1_t[:, L], nmr_t[:, L], nmr_t[:, L]))
            pvar = P.emit("vector", lambda e, L=L: e.scalar_tensor_tensor(
                rstd_t[:, L], pD[:, L], 1.0 / HH, tmp1_t[:, L],
                ALU.mult, ALU.subtract), (), ("sV", 1))
            P.emit("scalar", lambda e, L=L: e.activation(
                tmp1_t[:, L], rstd_t[:, L], AF.Ln, bias=c_eps[:], scale=1.0),
                [("sV", pvar)])
            pstd = P.emit("scalar", lambda e, L=L: e.activation(
                rstd_t[:, L], tmp1_t[:, L], AF.Exp, bias=c_lnhalf[:], scale=-0.5),
                (), ("sS", 1))
            chain = P.emit("vector", lambda e, L=L: e.scalar_tensor_tensor(
                nmr_t[:, L], nmr_t[:, L], -1.0, rstd_t[:, L], ALU.mult, ALU.mult),
                [("sS", pstd)], ("sV", 1))
            chain_v[sp] = chain
            if sp + 1 < NSTRIP:
                for M in range(EARLY):
                    emit_fc(sp + 1, M)
            pjm, pje = {}, {}
            for dM in range(DC):
                b = dM % 2
                semn = "dWC" if b % 2 == 0 else "dWD"
                dv = P.dma("sync", wpr_b[b][:], d_qpr[dM], semn,
                           [("sT", max(bufP.get(b, 0), w0[semn])),
                            ("dSCR", prep_scr)])
                wts = [(semn, dv), ("sV", pje.get(dM - 2, 0)),
                       ("sS", gel_last.get(dM % 2, 0) if dM < 2 else 0)]
                for k in range(HC):
                    for w_ in range(NHW):
                        ws = slice(512 * w_, 512 * (w_ + 1))
                        pjm[dM] = P.emit(
                            "tensor", lambda e, sp=sp, dM=dM, k=k, ws=ws: e.matmul(
                                (pA if dM % 2 == 0 else pB)[:, ws],
                                wpr_b[dM % 2][:, 128 * k:128 * (k + 1)],
                                g_t[gidx(sp, k)][:, ws],
                                start=(k == 0), stop=(k == HC - 1)),
                            wts if k == 0 and w_ == 0 else (),
                            ("sT", 1) if (k == HC - 1 and w_ == NHW - 1) else None)
                wstate[semn] = pjm[dM]
                bufP[b] = pjm[dM]
                P.emit("vector", lambda e, dM=dM, L=L: e.tensor_mul(
                    tmp1_t[:, L], (pA if dM % 2 == 0 else pB)[:, L], rstd_t[:, L]),
                    [("sT", pjm[dM]), ("sV", chain)])
                P.emit("vector", lambda e, dM=dM, L=L: e.scalar_tensor_tensor(
                    tmp1_t[:, L], nmr_t[:, L], c1_t[:, dM:dM + 1], tmp1_t[:, L],
                    ALU.mult, ALU.add))
                pje[dM] = P.emit("vector",
                    lambda e, dM=dM, c0=c0, L=L: e.scalar_tensor_tensor(
                        x_t[dM][:, c0:c0 + SW], tmp1_t[:, L], c2sb_t[:, dM:dM + 1],
                        x_t[dM][:, c0:c0 + SW], ALU.add, ALU.add),
                    (), ("sV", 1))
                evac_last[dM % 2] = pje[dM]
            ev_done = pje[DC - 1]
            prev_pje = pje
        P.emit("tensor", lambda e: e.nop(), [("sV", ev_done)], ("sT", 1))
        esF.close()
        return ev_done

    def attention(seq, xwait):
        uid[0] += 1
        tag = f"a{uid[0]}"
        bar = P.barrier()
        xn_done = outer_ln([("sT", bar["sT"]), ("dX", xwait)])
        esA = contextlib.ExitStack()
        asb = lambda n, sh, dt: esA.enter_context(nc.sbuf_tensor(n + tag, sh, dt))
        pPV = esA.enter_context(nc.psum_tensor(f"pPV{tag}", [65, 1024], F32))
        pE = esA.enter_context(nc.psum_tensor(f"pE{tag}", [128, 1024], F32))
        att_t = [asb(f"att{p}", [128, S], BF16) for p in range(DC)]
        va_t = [asb(f"va{h}", [128, IT * (E + 1)], BF16) for h in range(4)]
        qa_t = [asb(f"qa{i}", [65, S], BF16) for i in range(4)]
        ka_t = [asb(f"ka{i}", [65, S], BF16) for i in range(4)]
        sigb_t = asb("sigb", [128, S], BF16)
        esH = contextlib.ExitStack()
        hsb = lambda n, sh, dt: esH.enter_context(nc.sbuf_tensor(n + tag, sh, dt))
        vfm_t = hsb("vfm", [128, S], BF16)
        wqk_sb = [hsb(f"wqk{j}", [128, DC * 384], BF16) for j in range(2)]
        pt_t = [hsb(f"pt{i}", [128, 1024], BF16) for i in range(3)]
        mcA_t = hsb("mcA", [128, IT * NQ], F32)
        mcol_t = hsb("mcol2", [128, 2], F32)
        sigr_t = hsb("sigr", [1, LNW], F32)

        for i in range(4):
            ones_init = P.emit("vector", lambda e, i=i: e.memset(ka_t[i][64:65, :], 1.0))
        for h in range(4):
            for jt in range(IT):
                ones_init = P.emit("vector", lambda e, h=h, jt=jt: e.memset(
                    va_t[h][:, jt * 65 + 64:jt * 65 + 65], 1.0))
        ones_init = P.emit("vector", lambda e: e.engine_nop(), (), ("sV", 1))

        NIH = S // 1024 if S >= 1024 else 1
        IW = min(1024, S)
        slot_ap = [pA[:, 0:512], pA[:, 512:1024]]
        st_attn = {
            "slot_free": [xn_done, xn_done],
            "pB_free": 0,
            "pPV_free": 0,
            "qa_free": [0, 0],
            "va_free": [0, 0, 0, 0],
            "acopy": {}, "vmax_all": {}, "lastexp": {}, "attev": {},
            "gw": {}, "qk_ev": {}, "vfm_free": 0, "wqk_free": {},
            "gcc": 0, "exp_d": {}, "pv_d": {}, "pv_cp": 0, "pv_ln": 0,
        }
        A = st_attn

        def unit_A(hp):
            th = []
            def t_dma():
                w0 = [("sT", max(A["wqk_free"].get(hp - 2, 0), bar["sT"]))]
                sb = wqk_sb[hp % 2]
                dst = lambda off: sb[:].rearrange("p (c x) -> p c x", x=384)[:, :, off:off + 128]
                src3 = lambda t: t.rearrange("(c p) e -> p c (e)", p=128)[
                    :, :, 128 * hp:128 * (hp + 1)]
                sA, sB = ("gA", "gB") if hp % 2 == 0 else ("gB", "gA")
                gq = P.dma("gpsimd", dst(0), src3(d_wq), sA, w0)
                gk = P.dma("gpsimd", dst(128), src3(d_wk), sB, w0)
                gv = P.dma("gpsimd", dst(256), src3(d_wv), sA)
                A["gw"][hp] = {0: (sA, gq), 1: (sB, gk), 2: (sA, gv)}
            th.append(t_dma)
            lv = 2 * (hp % 2)
            evd = {}
            for part, woff in ((0, 0), (1, 128), (2, 256)):
                for q_ in range(NQ):
                    s_ = q_ % 2
                    def t_mm(part=part, woff=woff, q_=q_, s_=s_, hp=hp):
                        semn, gw = A["gw"][hp][part]
                        w = [(semn, gw), ("sV", xn_done),
                             ("sV", A["slot_free"][s_]),
                             ("sT", A["va_free"][lv] if part == 2 else
                              A["qa_free"][hp % 2])]
                        mm = None
                        for k in range(DC):
                            mm = P.emit("tensor", lambda e, k=k, hp=hp: e.matmul(
                                slot_ap[s_],
                                wqk_sb[hp % 2][:, 384 * k + woff:384 * k + woff + 128],
                                xn_t[k][:, 512 * q_:512 * (q_ + 1)],
                                start=(k == 0), stop=(k == DC - 1)),
                                w if k == 0 else (),
                                ("sT", 1) if k == DC - 1 else None)
                        A[f"mm{part}_{q_}"] = mm
                        if part == 2 and q_ == NQ - 1:
                            A["wqk_free"][hp] = mm
                    th.append(t_mm)
                    def t_ev(part=part, q_=q_, s_=s_, hp=hp, lv=lv):
                        qb = 2 * (hp % 2)
                        w = [("sT", A[f"mm{part}_{q_}"])]
                        qs = slice(512 * q_, 512 * (q_ + 1))
                        if part == 2:
                            ev = P.emit("vector", lambda e: e.tensor_single_scalar(
                                vfm_t[:, qs], slot_ap[s_], p_bv[:, hp:hp + 1],
                                ALU.add), w, ("sV", 1))
                        elif part == 0:
                            P.emit("vector", lambda e: e.tensor_scalar(
                                qa_t[qb][0:64, qs], slot_ap[s_][0:64, :], 0.125,
                                p_bq[0:64, hp:hp + 1], ALU.mult, ALU.add), w)
                            ev = P.emit("vector", lambda e: e.tensor_scalar(
                                qa_t[qb + 1][0:64, qs], slot_ap[s_][64:128, :], 0.125,
                                p_bq[64:128, hp:hp + 1], ALU.mult, ALU.add),
                                (), ("sV", 1))
                        else:
                            P.emit("vector", lambda e: e.tensor_single_scalar(
                                ka_t[qb][0:64, qs], slot_ap[s_][0:64, :],
                                p_bk[0:64, hp:hp + 1], ALU.add), w)
                            ev = P.emit("vector", lambda e: e.tensor_single_scalar(
                                ka_t[qb + 1][0:64, qs], slot_ap[s_][64:128, :],
                                p_bk[64:128, hp:hp + 1], ALU.add), (), ("sV", 1))
                        A["slot_free"][s_] = ev
                        if part == 1 and q_ == NQ - 1:
                            A["qk_ev"][hp] = ev
                    th.append(t_ev)
            for jt in range(IT):
                def t_tp(jt=jt, hp=hp, lv=lv):
                    s_ = jt % 2
                    pO = slot_ap[s_][:, 0:64].bitcast(BF16)
                    tp = P.emit("tensor", lambda e: e.transpose(
                        pO, vfm_t[:, 128 * jt:128 * (jt + 1)], c_idb[:]),
                        [("sV", A["slot_free"][s_]),
                         ("sT", max(A["va_free"][lv], A["va_free"][lv + 1]))],
                        ("sT", 1))
                    P.emit("vector", lambda e: e.tensor_copy(
                        va_t[lv][:, 65 * jt:65 * jt + 64], pO[:, 0:64]),
                        [("sT", tp), ("sV", ones_init)])
                    cp = P.emit("vector", lambda e: e.tensor_copy(
                        va_t[lv + 1][:, 65 * jt:65 * jt + 64], pO[:, 64:128]),
                        (), ("sV", 1))
                    A["slot_free"][s_] = cp
                th.append(t_tp)
            return th

        def unit_pass1(h):
            th = []
            hi = 2 * ((h // 2) % 2) + (h % 2)
            nch = [0]
            for it in range(IT):
                for c in range(NQ):
                    def t_c(it=it, c=c, hi=hi, h=h):
                        s_ = c % 2
                        w = [("sV", A["qk_ev"][h // 2]),
                             ("sV", A["slot_free"][s_])]
                        mm = P.emit("tensor", lambda e: e.matmul(
                            slot_ap[s_],
                            qa_t[hi][0:64, 128 * it:128 * (it + 1)],
                            ka_t[hi][0:64, 512 * c:512 * (c + 1)],
                            start=True, stop=True), w, ("sT", 1))
                        col = nch[0]
                        nch[0] += 1
                        vx = P.emit("vector", lambda e: e.reduce_max(
                            mcA_t[:, col:col + 1], slot_ap[s_], axis=AX.X,
                            negate=True), [("sT", mm), ("sT", A.get("arow_t", 0))],
                            ("sV", 1))
                        A["slot_free"][s_] = vx
                    th.append(t_c)
            def t_arow(h=h, hi=hi):
                P.emit("vector", lambda e: e.engine_nop())
                P.emit("vector", lambda e, n=nch[0]: e.tensor_reduce(
                    mcol_t[:, 0:1], mcA_t[:, 0:n], axis=AX.X, op=ALU.min))
                vl = P.emit("vector", lambda e: e.engine_nop(), (), ("sV", 1))
                tA = P.emit("tensor", lambda e: e.transpose(
                    pA[0:1, 0:128], mcol_t[:, 0:1], c_idf[:]),
                    [("sV", vl)], ("sT", 1))
                A["arow_t"] = tA
                P.emit("vector", lambda e: e.engine_nop(), [("sT", tA)])
                vA = P.emit("vector", lambda e: e.tensor_reduce(
                    mcol_t[0:1, 1:2], pA[0:1, 0:128], axis=AX.X, op=ALU.min),
                    (), ("sV", 1))
                A["slot_free"][0] = vA
                P.emit("vector", lambda e: e.engine_nop())
                P.emit("vector", lambda e: e.engine_nop())
                A["acopy"][h] = P.emit("vector", lambda e, hi=hi: e.tensor_single_scalar(
                    qa_t[hi][64:65, :], ka_t[hi][64:65, :], mcol_t[0:1, 1:2],
                    ALU.mult), (), ("sV", 1))
            th.append(t_arow)
            return th

        def unit_p2pv(h):
            th = []
            hp = h // 2
            hi = 2 * (hp % 2) + (h % 2)
            lva = hi
            exp_d, pv_d = A["exp_d"], A["pv_d"]
            LAG = 2
            for ih in range(NIH):
                pend = []
                for jt in range(IT):
                    def t_p2e(ih=ih, jt=jt, hi=hi, h=h):
                        cc = A["gcc"]
                        A["gcc"] += 1
                        A[f"cc{h}_{ih}_{jt}"] = cc
                        sl = pB if cc % 2 == 0 else pE
                        w = [("sV", A["acopy"][h]), ("sS", exp_d.get(cc - 2, 0))]
                        if jt == 0:
                            w += [("sV", A["pv_cp"]), ("sS", A["pv_ln"])]
                        mm = None
                        for w_ in range(IW // 512):
                            iw = slice(1024 * ih + 512 * w_,
                                       1024 * ih + 512 * (w_ + 1))
                            mm = P.emit("tensor", lambda e, iw=iw, w_=w_, sl=sl:
                                e.matmul(
                                    sl[:, 512 * w_:512 * (w_ + 1)],
                                    ka_t[hi][0:65, 128 * jt:128 * (jt + 1)],
                                    qa_t[hi][0:65, iw], start=True, stop=True),
                                w if w_ == 0 else (),
                                ("sT", 1) if w_ == IW // 512 - 1 else None)
                        ex = P.emit("scalar", lambda e, sl=sl, cc=cc: e.activation(
                            pt_t[cc % 3][:, 0:IW], sl[:, 0:IW], AF.Exp),
                            [("sT", mm), ("sT", pv_d.get(cc - 3, 0))], ("sS", 1))
                        exp_d[cc] = ex
                    def t_pv(ih=ih, jt=jt, lva=lva, h=h, hp=hp):
                        cc = A[f"cc{h}_{ih}_{jt}"]
                        pv = None
                        for w_ in range(IW // 512):
                            cw = slice(512 * w_, 512 * (w_ + 1))
                            pv = P.emit("tensor", lambda e, cw=cw, cc=cc, jt=jt:
                                e.matmul(
                                    pPV[:, cw], va_t[lva][:, 65 * jt:65 * jt + 65],
                                    pt_t[cc % 3][:, cw],
                                    start=(jt == 0), stop=(jt == IT - 1)),
                                [("sS", exp_d[cc])] if w_ == 0 else (),
                                ("sT", 1) if w_ == IW // 512 - 1 else None)
                        pv_d[cc] = pv
                        if jt == IT - 1:
                            A[f"pvh{h}_{ih}"] = pv
                            if ih == NIH - 1:
                                if h % 2 == 1:
                                    A["qa_free"][hp % 2] = pv
                                A["va_free"][lva] = pv
                    th.append(t_p2e)
                    pend.append(t_pv)
                    if len(pend) > LAG:
                        th.append(pend.pop(0))
                th.extend(pend)
                def t_fin(h=h, ih=ih):
                    pv = A[f"pvh{h}_{ih}"]
                    cs_ = slice(1024 * ih, 1024 * ih + IW)
                    dst = att_t[h // 2][64 * (h % 2):64 * (h % 2) + 64, cs_]
                    def _craw(e, dst=dst):
                        with nc.allow_low_precision(reason="bf16 attn out is "
                                                    "the target precision"):
                            return e.tensor_copy(dst, pPV[0:64, 0:IW])
                    cpv = P.emit("vector", _craw, [("sT", pv)], ("sV", 1))
                    A["pv_cp"] = cpv
                    lnv = P.emit("scalar", lambda e: e.activation(
                        sigr_t[0:1, 0:IW], pPV[64:65, 0:IW], AF.Ln),
                        [("sT", pv), ("gS", A.get("sig_gs", 0))], ("sS", 1))
                    A["pv_ln"] = lnv
                    def _recip(e, cs_=cs_):
                        with nc.allow_low_precision(reason="bf16 1/Sigma = one "
                                                    "more rounding of attn out"):
                            return e.activation(sigb_t[0:1, cs_], sigr_t[0:1, 0:IW],
                                                AF.Exp, bias=0.0, scale=-1.0)
                    vr = P.emit("scalar", _recip, (), ("sS", 1))
                    g1_ = P.dma("gpsimd", d_row[0:1, cs_], sigb_t[0:1, cs_], "gS",
                                [("sS", vr)])
                    g2_ = P.dma("gpsimd", sigb_t[:, cs_],
                                d_row[0:1, cs_].partition_broadcast(128), "gS",
                                [("gS", g1_)])
                    A["sig_gs"] = g2_
                    ev = P.emit("vector", lambda e, dst=dst, cs_=cs_, h=h:
                        e.tensor_mul(dst, dst,
                                     sigb_t[64 * (h % 2):64 * (h % 2) + 64, cs_]),
                        [("gS", g2_), ("sV", cpv)], ("sV", 1))
                    A["attev"][h] = ev
                th.append(t_fin)
            return th

        def run_units(*lists):
            lists = [l for l in lists if l]
            total = sum(len(l) for l in lists)
            idx = [0] * len(lists)
            for k in range(total):
                best, bi = -1.0, 0
                for i, l in enumerate(lists):
                    if idx[i] < len(l):
                        r = (len(l) - idx[i]) / len(l)
                        if r > best:
                            best, bi = r, i
                lists[bi][idx[bi]]()
                idx[bi] += 1

        run_units(unit_A(0))
        run_units(unit_pass1(0))
        for h in range(NH):
            post = unit_p2pv(h)
            pre = []
            if h + 1 < NH:
                if (h + 1) % 2 == 0:
                    pre += unit_A((h + 1) // 2)
                pre += unit_pass1(h + 1)
            run_units(pre, post)

        esH.close()
        wo_sb = [asb(f"wos{i}", [128, D], BF16) for i in range(DC)]
        gw = None
        for c in range(DC):
            gw = P.dma("gpsimd", wo_sb[c][:], d_woT[128 * c:128 * (c + 1), :], "gW",
                       [("sV", A["attev"][NH - 1])])
        woev = {}
        slot = 0
        for dM in range(DC):
            for q in range(NQ):
                pO = pA if slot % 2 == 0 else pB
                sl = pO[:, 512 * (slot // 2 % 2):512 * (slot // 2 % 2) + 512]
                wts = [("gW", gw), ("sV", woev.get(slot - 4, 0)),
                       ("sV", A["attev"][NH - 1])]
                for k in range(DC):
                    mm = P.emit("tensor",
                                lambda e, sl=sl, k=k, dM=dM, q=q: e.matmul(
                                    sl, wo_sb[k][:, 128 * dM:128 * (dM + 1)],
                                    att_t[k][:, 512 * q:512 * (q + 1)],
                                    start=(k == 0), stop=(k == DC - 1)),
                                wts if k == 0 else (),
                                ("sT", 1) if k == DC - 1 else None)
                woev[slot] = P.emit(
                    "vector", lambda e, sl=sl, dM=dM, q=q: e.scalar_tensor_tensor(
                        x_t[dM][:, 512 * q:512 * (q + 1)], sl, p_bo[:, dM:dM + 1],
                        x_t[dM][:, 512 * q:512 * (q + 1)], ALU.add, ALU.add),
                    [("sT", mm)], ("sV", 1))
                slot += 1
        done = woev[slot - 1]
        P.emit("tensor", lambda e: e.nop(), [("sV", done)], ("sT", 1))
        esA.close()
        return done

    prev_store = 0
    xl = xload0
    for seq in range(NSEQ):
        if seq > 0:
            for c in range(DC):
                xl = P.dma("sync", x_t[c][:], d_x[seq, 128 * c:128 * (c + 1), :],
                           "dX", [("dXO", prev_store), ("sV", P.cnt["sV"])])
        ffn(seq, xl)
        if dbg_after == 'f1':
            for c in range(DC):
                P.dma("sync", d_dbg[seq, 128 * c:128 * (c + 1), :], x_t[c][:],
                      "dXO", [("sV", P.cnt["sV"])])
        attention(seq, xl)
        if dbg_after == 'attn':
            for c in range(DC):
                P.dma("sync", d_dbg[seq, 128 * c:128 * (c + 1), :], x_t[c][:],
                      "dXO", [("sV", P.cnt["sV"])])
        ffn(seq, xl)
        if dbg_after == 'f2':
            for c in range(DC):
                P.dma("sync", d_dbg[seq, 128 * c:128 * (c + 1), :], x_t[c][:],
                      "dXO", [("sV", P.cnt["sV"])])
        for c in range(DC):
            prev_store = P.dma("sync", d_y[seq, 128 * c:128 * (c + 1), :],
                               x_t[c][:], "dXO", [("sV", P.cnt["sV"])])
    if cfg.get('dbg_qfc'):
        for M in range(HC):
            P.dma("sync", d_dbgq[M], d_qfc[M], "dXO", [("dSCR", prep_scr)])
        for M in range(DC):
            P.dma("sync", d_dbgp[M], d_qpr[M], "dXO", [("dSCR", prep_scr)])
        P.dma("sync", d_dbgc[:, 0:DC], c1_t[:], "dXO", [("sV", prep_done_V)])
        P.dma("sync", d_dbgc[:, DC:2 * DC], c2sb_t[:], "dXO", [("sV", prep_done_V)])
        P.dma("sync", d_dbgc[:, 2 * DC:2 * DC + 2], tau_t[:, 0:2], "dXO", [("sV", prep_done_V)])
        P.dma("sync", d_dbgc[:, 2 * DC + 2:2 * DC + 4], gacc_t[:, 400:402], "dXO", [("sV", prep_done_V)]) if False else None
    fin = P.cnt["dXO"]
    P.emit("sync", lambda e: e.nop(), [("dXO", fin)])

    block = es.enter_context(nc.Block())
    P.replay(block)
    es.close()
    return nc



def _layout_inputs(L0_attn_x, L0_attn_wq, L0_attn_bq, L0_attn_wk, L0_attn_bk,
                   L0_attn_wv, L0_attn_bv, L0_attn_wo, L0_attn_bo,
                   L0_ffn_W_fc, L0_ffn_ln_w, L0_ffn_ln_b, L0_ffn_W_proj,
                   n_cores, nseq, S):
    C = np.ascontiguousarray
    shared = {
        "wfcT": C(L0_ffn_W_fc.T),
        "wprT": C(L0_ffn_W_proj.T),
        "wq": C(L0_attn_wq.transpose(1, 0, 2).reshape(D, NH * E)),
        "wk": C(L0_attn_wk.transpose(1, 0, 2).reshape(D, NH * E)),
        "wv": C(L0_attn_wv.transpose(1, 0, 2).reshape(D, NH * E)),
        "woT": C(L0_attn_wo.T),
        "bq": C(L0_attn_bq.reshape(NH // 2, 128).T),
        "bk": C(L0_attn_bk.reshape(NH // 2, 128).T),
        "bv": C(L0_attn_bv.reshape(NH // 2, 128).T),
        "bo": C(L0_attn_bo.reshape(DC, 128).T),
        "lnw": C(L0_ffn_ln_w.reshape(HC, 128).T),
        "lnb": C(L0_ffn_ln_b.reshape(HC, 128).T),
        "idf": np.eye(128, dtype=np.float32),
        "idb": np.eye(128, dtype=BF),
        "onesb": np.ones((128, 128), dtype=BF),
        "eps": np.full((128, 1), 1e-5, np.float32),
        "lnhalf": np.full((128, 1), np.log(0.5), np.float32),
    }
    in_maps = []
    for c in range(n_cores):
        xs = L0_attn_x[c * nseq:(c + 1) * nseq]
        m = dict(shared)
        m["x"] = C(xs.transpose(0, 2, 1))
        in_maps.append(m)
    return in_maps


def kernel(**inputs):
    x = inputs["L0_attn_x"]
    B, S, _ = x.shape
    nseq = B // NCORES
    cfg = {"S": S, "NSEQ": nseq}
    nc = build_program(cfg)
    in_maps = _layout_inputs(n_cores=NCORES, nseq=nseq, S=S, **inputs)
    res = run_bass_kernel_spmd(nc, in_maps, core_ids=list(range(NCORES)))
    out = np.empty((B, S, D), np.float32)
    for c in range(NCORES):
        y = res.results[c]["y"]
        out[c * nseq:(c + 1) * nseq] = y.transpose(0, 2, 1)
    return out


# version 16 (final)
# speedup vs baseline: 1.0181x; 1.0181x over previous
import contextlib
import numpy as np
import ml_dtypes
import concourse.bass as bass
import concourse.mybir as mybir
from concourse.bass_utils import run_bass_kernel_spmd

F32 = mybir.dt.float32
BF16 = mybir.dt.bfloat16
AF = mybir.ActivationFunctionType
ALU = mybir.AluOpType
AX = mybir.AxisListType
BF = ml_dtypes.bfloat16

D, DC, HH, HC, NH, E = 1024, 8, 4096, 32, 16, 64
NCORES = 8
ENGINES = ("tensor", "vector", "scalar", "sync", "gpsimd")


class Plan:

    def __init__(self, nc, es):
        self.nc, self.es = nc, es
        self.ops = {k: [] for k in ENGINES}
        self.sems, self.cnt, self.memo = {}, {}, {}

    def sem(self, *names):
        for n in names:
            self.sems[n] = self.es.enter_context(self.nc.semaphore(n))
            self.cnt[n] = 0

    def emit(self, eng, fn, waits=(), inc=None):
        wl = []
        for s, v in waits:
            if v and v > 0 and self.memo.get((eng, s), -1) < v:
                wl.append((self.sems[s], v))
                self.memo[(eng, s)] = v
        rec_inc = None
        if inc is not None:
            s, k = inc
            self.cnt[s] += k
            rec_inc = (self.sems[s], k)
        self.ops[eng].append((fn, wl, rec_inc))
        return self.cnt[inc[0]] if inc else None

    def dma(self, eng, out, in_, sem, waits=()):
        return self.emit(eng, lambda e, o=out, i=in_: e.dma_start(out=o, in_=i),
                         waits, (sem, 16))

    def barrier(self):
        return {s: self.cnt[s] for s in ("sT", "sV", "sS")}

    def replay(self, block):
        for eng in ENGINES:
            recs = self.ops[eng]
            if not recs:
                continue
            def run(e, recs=recs):
                for fn, wl, inc in recs:
                    for h, v in wl:
                        e.wait_ge(h, v)
                    inst = fn(e)
                    if inc is not None:
                        inst.then_inc(inc[0], inc[1])
            getattr(block, eng)(run)


def build_program(cfg):
    S = cfg['S']
    NSEQ = cfg['NSEQ']
    SW = min(1024, S)
    NSTRIP = S // SW
    IT = S // 128
    NQ = S // 512
    assert NQ in (1, 4), 'S must be 512 or 2048'
    NHW = SW // 512
    dbg_after = cfg.get('dbg_after')

    nc = bass.Bass("TRN2", target_bir_lowering=False,
                   detect_race_conditions=False)

    d_x = nc.dram_tensor("x", [NSEQ, D, S], F32, kind="ExternalInput").ap()
    d_wfcT = nc.dram_tensor("wfcT", [D, HH], F32, kind="ExternalInput").ap()
    d_wprT = nc.dram_tensor("wprT", [HH, D], F32, kind="ExternalInput").ap()
    d_wq = nc.dram_tensor("wq", [D, NH * E], F32, kind="ExternalInput").ap()
    d_wk = nc.dram_tensor("wk", [D, NH * E], F32, kind="ExternalInput").ap()
    d_wv = nc.dram_tensor("wv", [D, NH * E], F32, kind="ExternalInput").ap()
    d_woT = nc.dram_tensor("woT", [D, D], F32, kind="ExternalInput").ap()
    d_bq = nc.dram_tensor("bq", [128, NH // 2], F32, kind="ExternalInput").ap()
    d_bk = nc.dram_tensor("bk", [128, NH // 2], F32, kind="ExternalInput").ap()
    d_bv = nc.dram_tensor("bv", [128, NH // 2], F32, kind="ExternalInput").ap()
    d_bo = nc.dram_tensor("bo", [128, DC], F32, kind="ExternalInput").ap()
    d_lnw = nc.dram_tensor("lnw", [128, HC], F32, kind="ExternalInput").ap()
    d_lnb = nc.dram_tensor("lnb", [128, HC], F32, kind="ExternalInput").ap()
    d_idf = nc.dram_tensor("idf", [128, 128], F32, kind="ExternalInput").ap()
    d_idb = nc.dram_tensor("idb", [128, 128], BF16, kind="ExternalInput").ap()
    d_onesb = nc.dram_tensor("onesb", [128, 128], BF16, kind="ExternalInput").ap()
    d_eps = nc.dram_tensor("eps", [128, 1], F32, kind="ExternalInput").ap()
    d_lnhalf = nc.dram_tensor("lnhalf", [128, 1], F32, kind="ExternalInput").ap()
    d_y = nc.dram_tensor("y", [NSEQ, D, S], F32, kind="ExternalOutput").ap()
    d_dbg = (nc.dram_tensor("dbg", [NSEQ, D, S], F32, kind="ExternalOutput").ap()
             if dbg_after else None)
    d_qfc = nc.dram_tensor("qfc", [HC, 128, D], BF16, kind="Internal").ap()
    if cfg.get('dbg_qfc'):
        d_dbgq = nc.dram_tensor("dbgq", [HC, 128, D], BF16, kind="ExternalOutput").ap()
        d_dbgp = nc.dram_tensor("dbgp", [DC, 128, HH], BF16, kind="ExternalOutput").ap()
        d_dbgc = nc.dram_tensor("dbgc", [128, 2 * DC + 4], F32, kind="ExternalOutput").ap()
    d_qpr = nc.dram_tensor("qpr", [DC, 128, HH], BF16, kind="Internal").ap()
    d_row = nc.dram_tensor("rowd", [1, S], BF16, kind="Internal").ap()
    d_g1 = nc.dram_tensor("gam", [1, 2], F32, kind="Internal").ap()
    d_arow = nc.dram_tensor("arow", [IT, 128], F32, kind="Internal").ap()

    es = contextlib.ExitStack()
    sbf = lambda n, sh, dt: es.enter_context(nc.sbuf_tensor(n, sh, dt))

    x_t = [sbf(f"x{c}", [128, S], F32) for c in range(DC)]
    xn_t = [sbf(f"xn{c}", [128, S], BF16) for c in range(DC)]
    c_idf = sbf("c_idf", [128, 128], F32)
    c_idb = sbf("c_idb", [128, 128], BF16)
    c_onesb = sbf("c_onesb", [128, 128], BF16)
    c_eps = sbf("c_eps", [128, 1], F32)
    c_lnhalf = sbf("c_lnhalf", [128, 1], F32)
    c_onesf = sbf("c_onesf", [128, 2], F32)
    p_bq = sbf("p_bq", [128, NH // 2], F32)
    p_bk = sbf("p_bk", [128, NH // 2], F32)
    p_bv = sbf("p_bv", [128, NH // 2], F32)
    p_bo = sbf("p_bo", [128, DC], F32)
    p_lnw = sbf("p_lnw", [128, HC], F32)
    p_lnb = sbf("p_lnb", [128, HC], BF16)
    c1_t = sbf("c1", [128, DC], F32)
    c2sb_t = sbf("c2sb", [128, DC], F32)
    tau_t = sbf("tau", [128, 2], F32)
    ntau_t = sbf("ntau", [128, 2], F32)
    sc1_t = sbf("sc1", [1, 2], F32)
    LNW = SW

    pA = es.enter_context(nc.psum_tensor("pA", [128, 1024], F32))
    pB = es.enter_context(nc.psum_tensor("pB", [128, 1024], F32))

    P = Plan(nc, es)
    P.sem("sT", "sV", "sS",
          "dX", "dXO", "dWA", "dWB", "dWC", "dWD", "dC", "dPA", "dPB", "dSCR",
          "gA", "gB", "gS", "gW")

    for dst, src in [(c_idf[:], d_idf), (c_idb[:], d_idb), (c_onesb[:], d_onesb),
                     (c_eps[:], d_eps), (c_lnhalf[:], d_lnhalf),
                     (p_bq[:], d_bq), (p_bk[:], d_bk), (p_bv[:], d_bv),
                     (p_bo[:], d_bo), (p_lnw[:], d_lnw)]:
        cD = P.dma("sync", dst, src, "dC")
    for c in range(DC):
        xload0 = P.dma("sync", x_t[c][:], d_x[0, 128 * c:128 * (c + 1), :], "dX")
    cG = P.dma("gpsimd", p_lnb[:], d_lnb, "gA")
    P.emit("vector", lambda e: e.tensor_scalar_mul(p_bq[:], p_bq[:], 0.125),
           [("dC", cD)], ("sV", 1))
    consts_v = P.emit("vector", lambda e: e.memset(c_onesf[:], 1.0), (), ("sV", 1))

    NCH = DC + HC
    wraw_es = contextlib.ExitStack()
    wraw_b = [wraw_es.enter_context(nc.sbuf_tensor(f"wraw{i}", [128, HH], F32))
              for i in range(2)]
    wqc_b = [wraw_es.enter_context(nc.sbuf_tensor(f"wqc{i}", [128, HH], BF16))
             for i in range(2)]
    negq_t = wraw_es.enter_context(nc.sbuf_tensor("negq", [128, HH], F32))
    gacc_t = wraw_es.enter_context(nc.sbuf_tensor("gacc", [128, 512], F32))
    gred_t = wraw_es.enter_context(nc.sbuf_tensor("gred", [128, 2], F32))
    gred2_t = wraw_es.enter_context(nc.sbuf_tensor("gred2", [2, 128], F32))

    def chunk_src(i):
        if i < DC:
            return d_wfcT[128 * i:128 * (i + 1), :], HH
        return d_wprT[128 * (i - DC):128 * (i - DC + 1), :], D

    p1v = {}
    seg_base = {0: 0, 1: 64}
    for i in range(NCH):
        semn = "dPA" if i % 2 == 0 else "dPB"
        src, w = chunk_src(i)
        dv = P.dma("sync", wraw_b[i % 2][:, 0:w], src, semn,
                   [("sV", p1v.get(i - 2, 0))])
        nseg = w // 512
        base = (i * 8) if i < DC else (64 + (i - DC) * 2)
        for s_ in range(nseg):
            p1v[i] = P.emit("vector",
                            lambda e, i=i, s_=s_, base=base: e.reduce_sum(
                                gacc_t[:, base + s_:base + s_ + 1],
                                wraw_b[i % 2][:, 512 * s_:512 * (s_ + 1)],
                                axis=AX.X, apply_absolute_value=True),
                            [(semn, dv)] if s_ == 0 else (),
                            ("sV", 1) if s_ == nseg - 1 else None)

    def vnop():
        P.emit("vector", lambda e: e.engine_nop())

    def tree_cols(dst_off, src_off, n):
        cur_off, cur_n = src_off, n
        while cur_n > 1:
            half = cur_n // 2
            vnop(); vnop()
            P.emit("vector", lambda e, a=cur_off, h=half: e.tensor_add(
                gacc_t[:, a:a + h], gacc_t[:, a:a + h], gacc_t[:, a + h:a + 2 * h]))
            if cur_n % 2:
                vnop(); vnop()
                P.emit("vector", lambda e, a=cur_off, h=half, cn=cur_n: e.tensor_add(
                    gacc_t[:, a:a + 1], gacc_t[:, a:a + 1],
                    gacc_t[:, a + cn - 1:a + cn]))
            cur_n = half
        if cur_off != dst_off:
            vnop(); vnop()
            P.emit("vector", lambda e, a=cur_off, d=dst_off: e.tensor_copy(
                gacc_t[:, d:d + 1], gacc_t[:, a:a + 1]))

    P.emit("vector", lambda e: e.engine_nop(), [("sV", p1v[NCH - 1])])
    tree_cols(400, 0, 64)
    tree_cols(401, 64, 64)
    p1red = P.emit("vector", lambda e: e.engine_nop(), (), ("sV", 1))
    p1t = P.emit("tensor", lambda e: e.transpose(
        pA[0:2, 0:128], gacc_t[:, 400:402], c_idf[:]),
        [("sV", p1red), ("sV", consts_v), ("dC", cD)], ("sT", 1))
    p1c = P.emit("vector", lambda e: e.tensor_copy(gred2_t[:], pA[0:2, 0:128]),
                 [("sT", p1t)], ("sV", 1))
    cur = 128
    while cur > 1:
        half = cur // 2
        vnop(); vnop()
        P.emit("vector", lambda e, h=half: e.tensor_add(
            gred2_t[:, 0:h], gred2_t[:, 0:h], gred2_t[:, h:2 * h]))
        cur = half
    vnop(); vnop()
    p1s = P.emit("vector", lambda e: e.engine_nop(), (), ("sV", 1))
    p1t2 = P.emit("tensor", lambda e: e.transpose(
        pA[0:1, 0:2], gred2_t[:, 0:1], c_idf[0:2, 0:2]), [("sV", p1s)], ("sT", 1))
    P.emit("vector", lambda e: e.tensor_scalar_mul(sc1_t[0:1, 0:2], pA[0:1, 0:2],
                                                   1.0 / (D * HH)), [("sT", p1t2)])
    vnop(); vnop()
    P.emit("vector", lambda e: e.tensor_scalar(sc1_t[0:1, 0:2], sc1_t[0:1, 0:2],
                                               1e-6, None, ALU.add))
    vnop(); vnop()
    ptau = P.emit("vector", lambda e: e.tensor_scalar_mul(sc1_t[0:1, 0:2],
                                                          sc1_t[0:1, 0:2], 0.5),
                  (), ("sV", 1))
    g1 = P.dma("gpsimd", d_g1[0:1, 0:2], sc1_t[0:1, 0:2], "gS", [("sV", ptau)])
    g2 = P.dma("gpsimd", tau_t[:, 0:2], d_g1[0:1, 0:2].partition_broadcast(128),
               "gS", [("gS", g1)])
    pntau = P.emit("vector", lambda e: e.tensor_scalar_mul(ntau_t[:, 0:2],
                                                           tau_t[:, 0:2], -1.0),
                   [("gS", g2)], ("sV", 1))

    prep_scr_fc_holder = [0]
    p2v, p2q, p2c2t, p2c1t, p2s = {}, {}, {}, {}, {}
    first_c2 = [True]
    for i in range(NCH):
        fc = i < DC
        semn = "dPA" if i % 2 == 0 else "dPB"
        src, w = chunk_src(i)
        dv = P.dma("sync", wraw_b[i % 2][:, 0:w], src, semn,
                   [("sV", p2v.get(i - 2, p1red))])
        col = 0 if fc else 1
        P.emit("vector",
               lambda e, i=i, w=w, col=col: e.tensor_single_scalar(
                   negq_t[:, 0:w], wraw_b[i % 2][:, 0:w],
                   ntau_t[:, col:col + 1], ALU.is_lt),
               [(semn, dv), ("sV", pntau)])
        p2q[i] = P.emit("vector",
                        lambda e, i=i, w=w, col=col: e.scalar_tensor_tensor(
                            wqc_b[i % 2][:, 0:w], wraw_b[i % 2][:, 0:w],
                            tau_t[:, col:col + 1], negq_t[:, 0:w],
                            ALU.is_gt, ALU.subtract),
                        [("dSCR", p2s.get(i - 2, 0)),
                         ("sT", p2c1t.get(i - 2, 0))], ("sV", 1))
        if fc:
            p2v[i] = p2q[i]
        else:
            hc = i - DC
            for dM in range(DC):
                p2c2t[i] = P.emit(
                    "tensor",
                    lambda e, i=i, dM=dM, hc=hc, f=first_c2[0]: e.matmul(
                        pA[:, dM:dM + 1], wqc_b[i % 2][:, 128 * dM:128 * (dM + 1)],
                        p_lnb[:, hc:hc + 1], start=f, stop=False),
                    [("sV", p2q[i]), ("gA", cG)], ("sT", 1))
                first_c2[0] = False
            p2v[i] = P.emit("vector",
                            lambda e, i=i, hc=hc: e.tensor_single_scalar(
                                wqc_b[i % 2][:, 0:D], wqc_b[i % 2][:, 0:D],
                                p_lnw[:, hc:hc + 1], ALU.mult),
                            [("sT", p2c2t[i])], ("sV", 1))
            for dM in range(DC):
                last = (hc == HC - 1 and dM == DC - 1)
                p2c1 = P.emit(
                    "tensor",
                    lambda e, i=i, dM=dM, last=last: e.matmul(
                        pA[:, DC + dM:DC + dM + 1],
                        wqc_b[i % 2][:, 128 * dM:128 * (dM + 1)],
                        c_onesb[:, 0:1], start=False, stop=last),
                    [("sV", p2v[i])], ("sT", 1))
            p2c1t[i] = p2c1
        if fc:
            for M in range(HC):
                scr = P.dma("sync", d_qfc[M, :, 128 * i:128 * (i + 1)],
                            wqc_b[i % 2][:, 128 * M:128 * (M + 1)], "dSCR",
                            [("sV", p2v[i])])
        else:
            hc = i - DC
            for M in range(DC):
                scr = P.dma("sync", d_qpr[M, :, 128 * hc:128 * (hc + 1)],
                            wqc_b[i % 2][:, 128 * M:128 * (M + 1)], "dSCR",
                            [("sV", p2v[i])])
        p2s[i] = P.cnt["dSCR"]
        if i == DC - 1:
            prep_scr_fc_holder[0] = P.cnt["dSCR"]
    prep_scr = P.cnt["dSCR"]
    P.emit("vector", lambda e: e.tensor_scalar_mul(c2sb_t[:], pA[:, 0:DC], 0.5),
           [("sT", p2c1)])
    pc12 = P.emit("vector", lambda e: e.tensor_copy(c1_t[:], pA[:, DC:2 * DC]),
                  (), ("sV", 1))
    prep_done_V = pc12
    prep_scr_fc = prep_scr_fc_holder[0]
    wraw_es.close()

    wstate = {"dWA": 0, "dWB": 0, "dWC": 0, "dWD": 0, "dPA": 0, "dPB": 0}

    def outer_ln(first_waits):
        uid[0] += 1
        esL = contextlib.ExitStack()
        lt = f"L{uid[0]}"
        lsb = lambda n, sh, dt: esL.enter_context(nc.sbuf_tensor(n + lt, sh, dt))
        mu_t = lsb("mu", [128, LNW], F32)
        rstd_t = lsb("rstd", [128, LNW], F32)
        nmr_t = lsb("nmr", [128, LNW], F32)
        tmp1_t = lsb("tmp1", [128, LNW], F32)
        xc_t = [lsb(f"xc{i}", [128, LNW], BF16) for i in range(2)]
        xq_t = [lsb(f"xq{i}", [128, LNW], BF16) for i in range(2)]
        xn_done = 0
        prev_pvar = 0
        for half in range(S // LNW):
            cs = slice(LNW * half, LNW * (half + 1))
            vmk = {}
            tmk = {}
            for c in range(DC):
                w = list(first_waits) if (half == 0 and c == 0) else []
                w.append(("sT", tmk.get(c - 2, 0)))
                P.emit("vector", lambda e, c=c, cs=cs: e.tensor_copy(
                    xc_t[c % 2][:], x_t[c][:, cs]), w)
                vmk[c] = P.emit("vector", lambda e, c=c: e.tensor_mul(
                    xq_t[c % 2][:], xc_t[c % 2][:], xc_t[c % 2][:]), (), ("sV", 1))
                for w_ in range(NHW):
                    ws = slice(512 * w_, 512 * (w_ + 1))
                    P.emit("tensor", lambda e, c=c, ws=ws: e.matmul(
                        pA[:, ws], c_onesb[:], xc_t[c % 2][:, ws],
                        start=(c == 0), stop=(c == DC - 1)),
                        [("sV", vmk[c]), ("sV", prev_pvar)])
                    tmk[c] = P.emit("tensor", lambda e, c=c, ws=ws: e.matmul(
                        pB[:, ws], c_onesb[:], xq_t[c % 2][:, ws],
                        start=(c == 0), stop=(c == DC - 1)), (), ("sT", 1))
            L = slice(0, LNW)
            P.emit("vector", lambda e, L=L: e.tensor_scalar_mul(
                mu_t[:, L], pA[:, L], 1.0 / D), [("sT", tmk[DC - 1])])
            P.emit("vector", lambda e, L=L: e.tensor_mul(
                tmp1_t[:, L], mu_t[:, L], mu_t[:, L]))
            pvar = P.emit("vector", lambda e, L=L: e.scalar_tensor_tensor(
                rstd_t[:, L], pB[:, L], 1.0 / D, tmp1_t[:, L],
                ALU.mult, ALU.subtract), (), ("sV", 1))
            prev_pvar = pvar
            pstd = P.emit("scalar", lambda e, L=L: e.activation(
                tmp1_t[:, L], rstd_t[:, L], AF.Sqrt, bias=c_eps[:], scale=1.0),
                [("sV", pvar)], ("sS", 1))
            P.emit("vector", lambda e, L=L: e.reciprocal(rstd_t[:, L], tmp1_t[:, L]),
                   [("sS", pstd)])
            P.emit("vector", lambda e, L=L: e.scalar_tensor_tensor(
                nmr_t[:, L], mu_t[:, L], -1.0, rstd_t[:, L], ALU.mult, ALU.mult))
            for c in range(DC):
                P.emit("vector", lambda e, c=c, cs=cs, L=L: e.tensor_mul(
                    tmp1_t[:, L], x_t[c][:, cs], rstd_t[:, L]))
                xn_done = P.emit("vector", lambda e, c=c, cs=cs, L=L: e.tensor_add(
                    xn_t[c][:, cs], tmp1_t[:, L], nmr_t[:, L]), (), ("sV", 1))
        esL.close()
        return xn_done

    uid = [0]

    def ffn(seq, xwait, EARLY=2):
        uid[0] += 1
        tag = f"f{uid[0]}"
        bar = P.barrier()
        xn_done = outer_ln([("sT", bar["sT"]), ("dX", xwait)])
        esF = contextlib.ExitStack()
        pC = esF.enter_context(nc.psum_tensor(f"pC{tag}", [128, 1024], F32))
        pD = esF.enter_context(nc.psum_tensor(f"pD{tag}", [128, 1024], F32))
        g_t = [esF.enter_context(nc.sbuf_tensor(f"g{m}{tag}", [128, SW], BF16))
               for m in range(HC + EARLY)]
        tmp1_t = esF.enter_context(nc.sbuf_tensor(f"tmp1{tag}", [128, SW], F32))
        rstd_t = esF.enter_context(nc.sbuf_tensor(f"rstd{tag}", [128, SW], F32))
        nmr_t = esF.enter_context(nc.sbuf_tensor(f"nmr{tag}", [128, SW], F32))
        XQN = max(EARLY, 2) + 1
        xq_t = [esF.enter_context(nc.sbuf_tensor(f"xq{i}{tag}", [128, SW], BF16))
                for i in range(XQN)]
        wfc_b = [esF.enter_context(nc.sbuf_tensor(f"wfc{i}{tag}", [128, D], BF16))
                 for i in range(4)]
        wpr_b = [esF.enter_context(nc.sbuf_tensor(f"wpr{i}{tag}", [128, HH], BF16))
                 for i in range(2)]
        bufT, bufP = {}, {}
        gidx = lambda sp, M: M + (HC if (M < EARLY and sp % 2 == 1) else 0)
        w0 = dict(wstate)
        fcm, gel, gsq, stm = {}, {}, {}, {}
        prev_pje = {}
        chain_v = {}
        gel_last = {}
        evac_last = {}

        def emit_fc(sp, M):
            b = M % 4
            semn = ("dWA", "dWB", "dPA", "dPB")[b]
            dv = P.dma("sync", wfc_b[b][:], d_qfc[M], semn,
                       [("sT", max(bufT.get(b, 0), w0[semn],
                                   bar["sT"])), ("dSCR", prep_scr_fc)])
            wts = [(semn, dv), ("sV", xn_done), ("sS", gel_last.get(M % 2, 0)),
                   ("sV", evac_last.get(M % 2, 0))]
            c0 = SW * sp
            for k in range(DC):
                for w_ in range(NHW):
                    ws = slice(512 * w_, 512 * (w_ + 1))
                    xw = slice(c0 + 512 * w_, c0 + 512 * (w_ + 1))
                    fcm[(sp, M)] = P.emit(
                        "tensor", lambda e, M=M, k=k, ws=ws, xw=xw: e.matmul(
                            (pA if M % 2 == 0 else pB)[:, ws],
                            wfc_b[M % 4][:, 128 * k:128 * (k + 1)],
                            xn_t[k][:, xw], start=(k == 0), stop=(k == DC - 1)),
                        wts if k == 0 and w_ == 0 else (),
                        ("sT", 1) if (k == DC - 1 and w_ == NHW - 1) else None)
            wstate[semn] = fcm[(sp, M)]
            bufT[b] = fcm[(sp, M)]
            gi = gidx(sp, M)
            gel[(sp, M)] = P.emit("scalar", lambda e, M=M, gi=gi: e.activation(
                g_t[gi][:], (pA if M % 2 == 0 else pB)[:, 0:SW], AF.Gelu),
                [("sT", fcm[(sp, M)])], ("sS", 1))
            gel_last[M % 2] = gel[(sp, M)]
            gsq[(sp, M)] = P.emit("vector", lambda e, M=M, gi=gi: e.tensor_mul(
                xq_t[M % XQN][:], g_t[gi][:], g_t[gi][:]),
                [("sS", gel[(sp, M)]), ("sT", stm.get((sp, M - XQN), 0))], ("sV", 1))

        def emit_stats(sp, M):
            gi = gidx(sp, M)
            w = [("sV", gsq[(sp, M)]),
                 ("sV", chain_v.get(sp - 1, 0) if M == 0 else 0)]
            for w_ in range(NHW):
                ws = slice(512 * w_, 512 * (w_ + 1))
                P.emit("tensor", lambda e, gi=gi, ws=ws: e.matmul(
                    pC[:, ws], c_onesb[:], g_t[gi][:, ws],
                    start=(M == 0), stop=(M == HC - 1)), w if w_ == 0 else ())
                stm[(sp, M)] = P.emit("tensor", lambda e, M=M, ws=ws: e.matmul(
                    pD[:, ws], c_onesb[:], xq_t[M % XQN][:, ws],
                    start=(M == 0), stop=(M == HC - 1)), (), ("sT", 1))

        ev_done = 0
        for sp in range(NSTRIP):
            c0 = SW * sp
            for M in range(HC):
                if not (sp > 0 and M < EARLY):
                    emit_fc(sp, M)
                emit_stats(sp, M)
            L = slice(0, SW)
            P.emit("vector", lambda e, L=L: e.tensor_scalar_mul(
                nmr_t[:, L], pC[:, L], 1.0 / HH), [("sT", stm[(sp, HC - 1)])])
            P.emit("vector", lambda e, L=L: e.tensor_mul(
                tmp1_t[:, L], nmr_t[:, L], nmr_t[:, L]))
            pvar = P.emit("vector", lambda e, L=L: e.scalar_tensor_tensor(
                rstd_t[:, L], pD[:, L], 1.0 / HH, tmp1_t[:, L],
                ALU.mult, ALU.subtract), (), ("sV", 1))
            P.emit("scalar", lambda e, L=L: e.activation(
                tmp1_t[:, L], rstd_t[:, L], AF.Ln, bias=c_eps[:], scale=1.0),
                [("sV", pvar)])
            pstd = P.emit("scalar", lambda e, L=L: e.activation(
                rstd_t[:, L], tmp1_t[:, L], AF.Exp, bias=c_lnhalf[:], scale=-0.5),
                (), ("sS", 1))
            chain = P.emit("vector", lambda e, L=L: e.scalar_tensor_tensor(
                nmr_t[:, L], nmr_t[:, L], -1.0, rstd_t[:, L], ALU.mult, ALU.mult),
                [("sS", pstd)], ("sV", 1))
            chain_v[sp] = chain
            if sp + 1 < NSTRIP:
                for M in range(EARLY):
                    emit_fc(sp + 1, M)
            pjm, pje = {}, {}
            for dM in range(DC):
                b = dM % 2
                semn = "dWC" if b % 2 == 0 else "dWD"
                dv = P.dma("sync", wpr_b[b][:], d_qpr[dM], semn,
                           [("sT", max(bufP.get(b, 0), w0[semn])),
                            ("dSCR", prep_scr)])
                wts = [(semn, dv), ("sV", pje.get(dM - 2, 0)),
                       ("sS", gel_last.get(dM % 2, 0) if dM < 2 else 0)]
                for k in range(HC):
                    for w_ in range(NHW):
                        ws = slice(512 * w_, 512 * (w_ + 1))
                        pjm[dM] = P.emit(
                            "tensor", lambda e, sp=sp, dM=dM, k=k, ws=ws: e.matmul(
                                (pA if dM % 2 == 0 else pB)[:, ws],
                                wpr_b[dM % 2][:, 128 * k:128 * (k + 1)],
                                g_t[gidx(sp, k)][:, ws],
                                start=(k == 0), stop=(k == HC - 1)),
                            wts if k == 0 and w_ == 0 else (),
                            ("sT", 1) if (k == HC - 1 and w_ == NHW - 1) else None)
                wstate[semn] = pjm[dM]
                bufP[b] = pjm[dM]
                P.emit("vector", lambda e, dM=dM, L=L: e.tensor_mul(
                    tmp1_t[:, L], (pA if dM % 2 == 0 else pB)[:, L], rstd_t[:, L]),
                    [("sT", pjm[dM]), ("sV", chain)])
                P.emit("vector", lambda e, dM=dM, L=L: e.scalar_tensor_tensor(
                    tmp1_t[:, L], nmr_t[:, L], c1_t[:, dM:dM + 1], tmp1_t[:, L],
                    ALU.mult, ALU.add))
                pje[dM] = P.emit("vector",
                    lambda e, dM=dM, c0=c0, L=L: e.scalar_tensor_tensor(
                        x_t[dM][:, c0:c0 + SW], tmp1_t[:, L], c2sb_t[:, dM:dM + 1],
                        x_t[dM][:, c0:c0 + SW], ALU.add, ALU.add),
                    (), ("sV", 1))
                evac_last[dM % 2] = pje[dM]
            ev_done = pje[DC - 1]
            prev_pje = pje
        P.emit("tensor", lambda e: e.nop(), [("sV", ev_done)], ("sT", 1))
        esF.close()
        return ev_done

    def attention(seq, xwait):
        uid[0] += 1
        tag = f"a{uid[0]}"
        bar = P.barrier()
        xn_done = outer_ln([("sT", bar["sT"]), ("dX", xwait)])
        esA = contextlib.ExitStack()
        asb = lambda n, sh, dt: esA.enter_context(nc.sbuf_tensor(n + tag, sh, dt))
        pPV = esA.enter_context(nc.psum_tensor(f"pPV{tag}", [65, 1024], F32))
        pE = esA.enter_context(nc.psum_tensor(f"pE{tag}", [128, 1024], F32))
        att_t = [asb(f"att{p}", [128, S], BF16) for p in range(DC)]
        va_t = [asb(f"va{h}", [128, IT * (E + 1)], BF16) for h in range(4)]
        qa_t = [asb(f"qa{i}", [65, S], BF16) for i in range(4)]
        ka_t = [asb(f"ka{i}", [65, S], BF16) for i in range(4)]
        sigb_t = asb("sigb", [128, S], BF16)
        esH = contextlib.ExitStack()
        hsb = lambda n, sh, dt: esH.enter_context(nc.sbuf_tensor(n + tag, sh, dt))
        vfm_t = hsb("vfm", [128, S], BF16)
        wqk_sb = [hsb(f"wqk{j}", [128, DC * 384], BF16) for j in range(2)]
        pt_t = [hsb(f"pt{i}", [128, 1024], BF16) for i in range(3)]
        mcA_t = hsb("mcA", [128, IT * NQ], F32)
        mcol_t = hsb("mcol2", [128, 2], F32)
        sigr_t = hsb("sigr", [1, LNW], F32)

        for i in range(4):
            ones_init = P.emit("vector", lambda e, i=i: e.memset(ka_t[i][64:65, :], 1.0))
        for h in range(4):
            for jt in range(IT):
                ones_init = P.emit("vector", lambda e, h=h, jt=jt: e.memset(
                    va_t[h][:, jt * 65 + 64:jt * 65 + 65], 1.0))
        ones_init = P.emit("vector", lambda e: e.engine_nop(), (), ("sV", 1))

        NIH = S // 1024 if S >= 1024 else 1
        IW = min(1024, S)
        slot_ap = [pA[:, 0:512], pA[:, 512:1024]]
        st_attn = {
            "slot_free": [xn_done, xn_done],
            "pB_free": 0,
            "pPV_free": 0,
            "qa_free": [0, 0],
            "va_free": [0, 0, 0, 0],
            "acopy": {}, "vmax_all": {}, "lastexp": {}, "attev": {},
            "gw": {}, "qk_ev": {}, "vfm_free": 0, "wqk_free": {},
            "gcc": 0, "exp_d": {}, "pv_d": {}, "pv_cp": 0, "pv_ln": 0,
        }
        A = st_attn

        def unit_A(hp):
            th = []
            def t_dma():
                w0 = [("sT", max(A["wqk_free"].get(hp - 2, 0), bar["sT"]))]
                sb = wqk_sb[hp % 2]
                dst = lambda off: sb[:].rearrange("p (c x) -> p c x", x=384)[:, :, off:off + 128]
                src3 = lambda t: t.rearrange("(c p) e -> p c (e)", p=128)[
                    :, :, 128 * hp:128 * (hp + 1)]
                sA, sB = ("gA", "gB") if hp % 2 == 0 else ("gB", "gA")
                gq = P.dma("gpsimd", dst(0), src3(d_wq), sA, w0)
                gk = P.dma("gpsimd", dst(128), src3(d_wk), sB, w0)
                gv = P.dma("gpsimd", dst(256), src3(d_wv), sA)
                A["gw"][hp] = {0: (sA, gq), 1: (sB, gk), 2: (sA, gv)}
            th.append(t_dma)
            lv = 2 * (hp % 2)
            evd = {}
            for part, woff in ((0, 0), (1, 128), (2, 256)):
                for q_ in range(NQ):
                    s_ = q_ % 2
                    def t_mm(part=part, woff=woff, q_=q_, s_=s_, hp=hp):
                        semn, gw = A["gw"][hp][part]
                        w = [(semn, gw), ("sV", xn_done),
                             ("sV", A["slot_free"][s_]),
                             ("sT", A["va_free"][lv] if part == 2 else
                              A["qa_free"][hp % 2])]
                        mm = None
                        for k in range(DC):
                            mm = P.emit("tensor", lambda e, k=k, hp=hp: e.matmul(
                                slot_ap[s_],
                                wqk_sb[hp % 2][:, 384 * k + woff:384 * k + woff + 128],
                                xn_t[k][:, 512 * q_:512 * (q_ + 1)],
                                start=(k == 0), stop=(k == DC - 1)),
                                w if k == 0 else (),
                                ("sT", 1) if k == DC - 1 else None)
                        A[f"mm{part}_{q_}"] = mm
                        if part == 2 and q_ == NQ - 1:
                            A["wqk_free"][hp] = mm
                    th.append(t_mm)
                    def t_ev(part=part, q_=q_, s_=s_, hp=hp, lv=lv):
                        qb = 2 * (hp % 2)
                        w = [("sT", A[f"mm{part}_{q_}"])]
                        qs = slice(512 * q_, 512 * (q_ + 1))
                        if part == 2:
                            ev = P.emit("vector", lambda e: e.tensor_single_scalar(
                                vfm_t[:, qs], slot_ap[s_], p_bv[:, hp:hp + 1],
                                ALU.add), w, ("sV", 1))
                        elif part == 0:
                            P.emit("vector", lambda e: e.tensor_scalar(
                                qa_t[qb][0:64, qs], slot_ap[s_][0:64, :], 0.125,
                                p_bq[0:64, hp:hp + 1], ALU.mult, ALU.add), w)
                            ev = P.emit("vector", lambda e: e.tensor_scalar(
                                qa_t[qb + 1][0:64, qs], slot_ap[s_][64:128, :], 0.125,
                                p_bq[64:128, hp:hp + 1], ALU.mult, ALU.add),
                                (), ("sV", 1))
                        else:
                            P.emit("vector", lambda e: e.tensor_single_scalar(
                                ka_t[qb][0:64, qs], slot_ap[s_][0:64, :],
                                p_bk[0:64, hp:hp + 1], ALU.add), w)
                            ev = P.emit("vector", lambda e: e.tensor_single_scalar(
                                ka_t[qb + 1][0:64, qs], slot_ap[s_][64:128, :],
                                p_bk[64:128, hp:hp + 1], ALU.add), (), ("sV", 1))
                        A["slot_free"][s_] = ev
                        if part == 1 and q_ == NQ - 1:
                            A["qk_ev"][hp] = ev
                    th.append(t_ev)
            for jt in range(IT):
                def t_tp(jt=jt, hp=hp, lv=lv):
                    s_ = jt % 2
                    pO = slot_ap[s_][:, 0:64].bitcast(BF16)
                    tp = P.emit("tensor", lambda e: e.transpose(
                        pO, vfm_t[:, 128 * jt:128 * (jt + 1)], c_idb[:]),
                        [("sV", A["slot_free"][s_]),
                         ("sT", max(A["va_free"][lv], A["va_free"][lv + 1]))],
                        ("sT", 1))
                    P.emit("vector", lambda e: e.tensor_copy(
                        va_t[lv][:, 65 * jt:65 * jt + 64], pO[:, 0:64]),
                        [("sT", tp), ("sV", ones_init)])
                    cp = P.emit("vector", lambda e: e.tensor_copy(
                        va_t[lv + 1][:, 65 * jt:65 * jt + 64], pO[:, 64:128]),
                        (), ("sV", 1))
                    A["slot_free"][s_] = cp
                    A["last_tp"] = tp
                th.append(t_tp)
            return th

        def unit_pass1(h):
            th = []
            hi = 2 * ((h // 2) % 2) + (h % 2)
            nch = [0]
            for it in range(IT):
                for c in range(NQ):
                    def t_c(it=it, c=c, hi=hi, h=h):
                        s_ = c % 2
                        w = [("sV", A["qk_ev"][h // 2]),
                             ("sV", A["slot_free"][s_])]
                        mm = P.emit("tensor", lambda e: e.matmul(
                            slot_ap[s_],
                            qa_t[hi][0:64, 128 * it:128 * (it + 1)],
                            ka_t[hi][0:64, 512 * c:512 * (c + 1)],
                            start=True, stop=True), w, ("sT", 1))
                        col = nch[0]
                        nch[0] += 1
                        vx = P.emit("vector", lambda e: e.reduce_max(
                            mcA_t[:, col:col + 1], slot_ap[s_], axis=AX.X,
                            negate=True), [("sT", mm), ("sT", A.get("arow_t", 0))],
                            ("sV", 1))
                        A["slot_free"][s_] = vx
                    th.append(t_c)
            def t_arow(h=h, hi=hi):
                P.emit("vector", lambda e: e.engine_nop())
                P.emit("vector", lambda e, n=nch[0]: e.tensor_reduce(
                    mcol_t[:, 0:1], mcA_t[:, 0:n], axis=AX.X, op=ALU.min))
                vl = P.emit("vector", lambda e: e.engine_nop(), (), ("sV", 1))
                tA = P.emit("tensor", lambda e: e.transpose(
                    pA[0:1, 0:128], mcol_t[:, 0:1], c_idf[:]),
                    [("sV", vl)], ("sT", 1))
                A["arow_t"] = tA
                P.emit("vector", lambda e: e.engine_nop(), [("sT", tA)])
                vA = P.emit("vector", lambda e: e.tensor_reduce(
                    mcol_t[0:1, 1:2], pA[0:1, 0:128], axis=AX.X, op=ALU.min),
                    (), ("sV", 1))
                A["slot_free"][0] = vA
                P.emit("vector", lambda e: e.engine_nop())
                P.emit("vector", lambda e: e.engine_nop())
                A["acopy"][h] = P.emit("vector", lambda e, hi=hi: e.tensor_single_scalar(
                    qa_t[hi][64:65, :], ka_t[hi][64:65, :], mcol_t[0:1, 1:2],
                    ALU.mult), (), ("sV", 1))
            th.append(t_arow)
            return th

        def unit_p2pv(h):
            th = []
            hp = h // 2
            hi = 2 * (hp % 2) + (h % 2)
            lva = hi
            exp_d, pv_d = A["exp_d"], A["pv_d"]
            LAG = 2
            for ih in range(NIH):
                pend = []
                for jt in range(IT):
                    def t_p2e(ih=ih, jt=jt, hi=hi, h=h):
                        cc = A["gcc"]
                        A["gcc"] += 1
                        A[f"cc{h}_{ih}_{jt}"] = cc
                        sl = pB if cc % 2 == 0 else pE
                        w = [("sV", A["acopy"][h]), ("sS", exp_d.get(cc - 2, 0))]
                        if jt == 0:
                            w += [("sV", A["pv_cp"]), ("sS", A["pv_ln"])]
                        mm = None
                        for w_ in range(IW // 512):
                            iw = slice(1024 * ih + 512 * w_,
                                       1024 * ih + 512 * (w_ + 1))
                            mm = P.emit("tensor", lambda e, iw=iw, w_=w_, sl=sl:
                                e.matmul(
                                    sl[:, 512 * w_:512 * (w_ + 1)],
                                    ka_t[hi][0:65, 128 * jt:128 * (jt + 1)],
                                    qa_t[hi][0:65, iw], start=True, stop=True),
                                w if w_ == 0 else (),
                                ("sT", 1) if w_ == IW // 512 - 1 else None)
                        ex = P.emit("scalar", lambda e, sl=sl, cc=cc: e.activation(
                            pt_t[cc % 3][:, 0:IW], sl[:, 0:IW], AF.Exp),
                            [("sT", mm), ("sT", pv_d.get(cc - 3, 0))], ("sS", 1))
                        exp_d[cc] = ex
                    def t_pv(ih=ih, jt=jt, lva=lva, h=h, hp=hp):
                        cc = A[f"cc{h}_{ih}_{jt}"]
                        pv = None
                        for w_ in range(IW // 512):
                            cw = slice(512 * w_, 512 * (w_ + 1))
                            pv = P.emit("tensor", lambda e, cw=cw, cc=cc, jt=jt:
                                e.matmul(
                                    pPV[:, cw], va_t[lva][:, 65 * jt:65 * jt + 65],
                                    pt_t[cc % 3][:, cw],
                                    start=(jt == 0), stop=(jt == IT - 1)),
                                [("sS", exp_d[cc])] if w_ == 0 else (),
                                ("sT", 1) if w_ == IW // 512 - 1 else None)
                        pv_d[cc] = pv
                        if jt == IT - 1:
                            A[f"pvh{h}_{ih}"] = pv
                            if ih == NIH - 1:
                                if h % 2 == 1:
                                    A["qa_free"][hp % 2] = pv
                                A["va_free"][lva] = pv
                    th.append(t_p2e)
                    pend.append(t_pv)
                    if len(pend) > LAG:
                        th.append(pend.pop(0))
                th.extend(pend)
                def t_fin(h=h, ih=ih):
                    pv = A[f"pvh{h}_{ih}"]
                    cs_ = slice(1024 * ih, 1024 * ih + IW)
                    dst = att_t[h // 2][64 * (h % 2):64 * (h % 2) + 64, cs_]
                    def _craw(e, dst=dst):
                        with nc.allow_low_precision(reason="bf16 attn out is "
                                                    "the target precision"):
                            return e.tensor_copy(dst, pPV[0:64, 0:IW])
                    cpv = P.emit("vector", _craw, [("sT", pv)], ("sV", 1))
                    A["pv_cp"] = cpv
                    lnv = P.emit("scalar", lambda e: e.activation(
                        sigr_t[0:1, 0:IW], pPV[64:65, 0:IW], AF.Ln),
                        [("sT", pv), ("gS", A.get("sig_gs", 0))], ("sS", 1))
                    A["pv_ln"] = lnv
                    def _recip(e, cs_=cs_):
                        with nc.allow_low_precision(reason="bf16 1/Sigma = one "
                                                    "more rounding of attn out"):
                            return e.activation(sigb_t[0:1, cs_], sigr_t[0:1, 0:IW],
                                                AF.Exp, bias=0.0, scale=-1.0)
                    vr = P.emit("scalar", _recip, (), ("sS", 1))
                    g1_ = P.dma("gpsimd", d_row[0:1, cs_], sigb_t[0:1, cs_], "gS",
                                [("sS", vr)])
                    g2_ = P.dma("gpsimd", sigb_t[:, cs_],
                                d_row[0:1, cs_].partition_broadcast(128), "gS",
                                [("gS", g1_)])
                    A["sig_gs"] = g2_
                    ev = P.emit("vector", lambda e, dst=dst, cs_=cs_, h=h:
                        e.tensor_mul(dst, dst,
                                     sigb_t[64 * (h % 2):64 * (h % 2) + 64, cs_]),
                        [("gS", g2_), ("sV", cpv)], ("sV", 1))
                    A["attev"][h] = ev
                th.append(t_fin)
            return th

        def run_units(*lists):
            lists = [l for l in lists if l]
            total = sum(len(l) for l in lists)
            idx = [0] * len(lists)
            for k in range(total):
                best, bi = -1.0, 0
                for i, l in enumerate(lists):
                    if idx[i] < len(l):
                        r = (len(l) - idx[i]) / len(l)
                        if r > best:
                            best, bi = r, i
                lists[bi][idx[bi]]()
                idx[bi] += 1

        run_units(unit_A(0))
        run_units(unit_pass1(0))
        au_next = []
        for h in range(NH):
            post = unit_p2pv(h)
            pre = []
            extra = []
            if h + 2 < NH and (h + 2) % 2 == 0:
                au_next = unit_A((h + 2) // 2)
                extra = [au_next.pop(0)]
            if h + 1 < NH:
                if (h + 1) % 2 == 0:
                    pre += au_next
                pre += unit_pass1(h + 1)
            run_units(extra + pre, post)

        esH.close()
        wo_sb = [asb(f"wos{i}", [128, D], BF16) for i in range(DC)]
        wo_guard = max(A["wqk_free"][NH // 2 - 1], A["last_tp"])
        gw = None
        for c in range(DC):
            gw = P.dma("gpsimd", wo_sb[c][:], d_woT[128 * c:128 * (c + 1), :],
                       "gW", [("sT", wo_guard)] if c == 0 else ())
        woev = {}
        slot = 0
        for dM in range(DC):
            for q in range(NQ):
                pO = pA if slot % 2 == 0 else pB
                sl = pO[:, 512 * (slot // 2 % 2):512 * (slot // 2 % 2) + 512]
                wts = [("gW", gw), ("sV", woev.get(slot - 4, 0)),
                       ("sV", A["attev"][NH - 1])]
                for k in range(DC):
                    mm = P.emit("tensor",
                                lambda e, sl=sl, k=k, dM=dM, q=q: e.matmul(
                                    sl, wo_sb[k][:, 128 * dM:128 * (dM + 1)],
                                    att_t[k][:, 512 * q:512 * (q + 1)],
                                    start=(k == 0), stop=(k == DC - 1)),
                                wts if k == 0 else (),
                                ("sT", 1) if k == DC - 1 else None)
                woev[slot] = P.emit(
                    "vector", lambda e, sl=sl, dM=dM, q=q: e.scalar_tensor_tensor(
                        x_t[dM][:, 512 * q:512 * (q + 1)], sl, p_bo[:, dM:dM + 1],
                        x_t[dM][:, 512 * q:512 * (q + 1)], ALU.add, ALU.add),
                    [("sT", mm)], ("sV", 1))
                slot += 1
        done = woev[slot - 1]
        P.emit("tensor", lambda e: e.nop(), [("sV", done)], ("sT", 1))
        esA.close()
        return done

    prev_store = 0
    xl = xload0
    for seq in range(NSEQ):
        if seq > 0:
            for c in range(DC):
                xl = P.dma("sync", x_t[c][:], d_x[seq, 128 * c:128 * (c + 1), :],
                           "dX", [("dXO", prev_store), ("sV", P.cnt["sV"])])
        ffn(seq, xl)
        if dbg_after == 'f1':
            for c in range(DC):
                P.dma("sync", d_dbg[seq, 128 * c:128 * (c + 1), :], x_t[c][:],
                      "dXO", [("sV", P.cnt["sV"])])
        attention(seq, xl)
        if dbg_after == 'attn':
            for c in range(DC):
                P.dma("sync", d_dbg[seq, 128 * c:128 * (c + 1), :], x_t[c][:],
                      "dXO", [("sV", P.cnt["sV"])])
        ffn(seq, xl)
        if dbg_after == 'f2':
            for c in range(DC):
                P.dma("sync", d_dbg[seq, 128 * c:128 * (c + 1), :], x_t[c][:],
                      "dXO", [("sV", P.cnt["sV"])])
        for c in range(DC):
            prev_store = P.dma("sync", d_y[seq, 128 * c:128 * (c + 1), :],
                               x_t[c][:], "dXO", [("sV", P.cnt["sV"])])
    if cfg.get('dbg_qfc'):
        for M in range(HC):
            P.dma("sync", d_dbgq[M], d_qfc[M], "dXO", [("dSCR", prep_scr)])
        for M in range(DC):
            P.dma("sync", d_dbgp[M], d_qpr[M], "dXO", [("dSCR", prep_scr)])
        P.dma("sync", d_dbgc[:, 0:DC], c1_t[:], "dXO", [("sV", prep_done_V)])
        P.dma("sync", d_dbgc[:, DC:2 * DC], c2sb_t[:], "dXO", [("sV", prep_done_V)])
        P.dma("sync", d_dbgc[:, 2 * DC:2 * DC + 2], tau_t[:, 0:2], "dXO", [("sV", prep_done_V)])
        P.dma("sync", d_dbgc[:, 2 * DC + 2:2 * DC + 4], gacc_t[:, 400:402], "dXO", [("sV", prep_done_V)]) if False else None
    fin = P.cnt["dXO"]
    P.emit("sync", lambda e: e.nop(), [("dXO", fin)])

    block = es.enter_context(nc.Block())
    P.replay(block)
    es.close()
    return nc



def _layout_inputs(L0_attn_x, L0_attn_wq, L0_attn_bq, L0_attn_wk, L0_attn_bk,
                   L0_attn_wv, L0_attn_bv, L0_attn_wo, L0_attn_bo,
                   L0_ffn_W_fc, L0_ffn_ln_w, L0_ffn_ln_b, L0_ffn_W_proj,
                   n_cores, nseq, S):
    C = np.ascontiguousarray
    shared = {
        "wfcT": C(L0_ffn_W_fc.T),
        "wprT": C(L0_ffn_W_proj.T),
        "wq": C(L0_attn_wq.transpose(1, 0, 2).reshape(D, NH * E)),
        "wk": C(L0_attn_wk.transpose(1, 0, 2).reshape(D, NH * E)),
        "wv": C(L0_attn_wv.transpose(1, 0, 2).reshape(D, NH * E)),
        "woT": C(L0_attn_wo.T),
        "bq": C(L0_attn_bq.reshape(NH // 2, 128).T),
        "bk": C(L0_attn_bk.reshape(NH // 2, 128).T),
        "bv": C(L0_attn_bv.reshape(NH // 2, 128).T),
        "bo": C(L0_attn_bo.reshape(DC, 128).T),
        "lnw": C(L0_ffn_ln_w.reshape(HC, 128).T),
        "lnb": C(L0_ffn_ln_b.reshape(HC, 128).T),
        "idf": np.eye(128, dtype=np.float32),
        "idb": np.eye(128, dtype=BF),
        "onesb": np.ones((128, 128), dtype=BF),
        "eps": np.full((128, 1), 1e-5, np.float32),
        "lnhalf": np.full((128, 1), np.log(0.5), np.float32),
    }
    in_maps = []
    for c in range(n_cores):
        xs = L0_attn_x[c * nseq:(c + 1) * nseq]
        m = dict(shared)
        m["x"] = C(xs.transpose(0, 2, 1))
        in_maps.append(m)
    return in_maps


def kernel(**inputs):
    x = inputs["L0_attn_x"]
    B, S, _ = x.shape
    nseq = B // NCORES
    cfg = {"S": S, "NSEQ": nseq}
    nc = build_program(cfg)
    in_maps = _layout_inputs(n_cores=NCORES, nseq=nseq, S=S, **inputs)
    res = run_bass_kernel_spmd(nc, in_maps, core_ids=list(range(NCORES)))
    out = np.empty((B, S, D), np.float32)
    for c in range(NCORES):
        y = res.results[c]["y"]
        out[c * nseq:(c + 1) * nseq] = y.transpose(0, 2, 1)
    return out
